# version 18
# baseline (speedup 1.0000x reference)
import numpy as np
import ml_dtypes
import concourse.bass as bass
import concourse.mybir as mybir
from concourse.bass_utils import run_bass_kernel_spmd

F32 = mybir.dt.float32
BF16 = mybir.dt.bfloat16
AF = mybir.ActivationFunctionType
ALU = mybir.AluOpType
AX = mybir.AxisListType

P = 128
HCH = 8
TCORE = 4096
NT = 32
NQ = 4
QT = 1024
CH = 512
N16 = 16
NPASS = 10
PASS_ORDER = [8, 9, 0, 1, 2, 3, 4, 5, 6, 7]
BF = ml_dtypes.bfloat16
RACECHECK = []


def _ap_range(a):
    try:
        t = a.tensor
        name = t.name
        import concourse.mybir as _mb
        esz = _mb.dt.size(a.dtype)
        off = a.offset if isinstance(a.offset, int) else 0
        dims = [(int(s), int(c)) for s, c in list(a.ap)]
        if not dims:
            return (name, str(a.space), 0, 128, off * esz, off * esz + esz)
        rowstep, pcnt = dims[0]
        if rowstep <= 0:
            rowstep = 1
        plo = off // rowstep
        flo = off % rowstep
        ext = 0
        for stp, cnt in dims[1:]:
            if cnt > 1:
                ext += abs(stp) * (cnt - 1)
        return (name, str(a.space), plo, plo + pcnt, flo * esz,
                (flo + ext + 1) * esz)
    except Exception:
        return None


def build_program():
    nc = bass.Bass("TRN2", target_bir_lowering=False, debug=False)

    x_d = nc.dram_tensor("x", [TCORE, 1024], F32, kind="ExternalInput").ap()
    ws_d = nc.dram_tensor("wstream", [NPASS, 3, P, 4096], F32, kind="ExternalInput").ap()
    xpw_d = nc.dram_tensor("xpw", [P, 768], F32, kind="ExternalInput").ap()
    dtw_d = nc.dram_tensor("dtw", [64, 1024], F32, kind="ExternalInput").ap()
    alog_d = nc.dram_tensor("alog", [P, 128], F32, kind="ExternalInput").ap()
    dvec_d = nc.dram_tensor("dvec", [P, 8], F32, kind="ExternalInput").ap()
    dtb_d = nc.dram_tensor("dtb", [P, 8], F32, kind="ExternalInput").ap()
    gw_d = nc.dram_tensor("gw", [P, 64], F32, kind="ExternalInput").ap()
    lb_d = nc.dram_tensor("lb", [1, 8], F32, kind="ExternalInput").ap()
    identb_d = nc.dram_tensor("identb", [P, P], BF16, kind="ExternalInput").ap()
    identf_d = nc.dram_tensor("identf", [P, P], F32, kind="ExternalInput").ap()
    onesb_d = nc.dram_tensor("onesb", [P, 1], BF16, kind="ExternalInput").ap()
    out_d = nc.dram_tensor("out", [TCORE, 1024], F32, kind="ExternalOutput").ap()
    wcache_d = nc.dram_tensor("wcache", [NPASS, 3, P, 4096], BF16).ap()
    bcb_d = nc.dram_tensor("bcb", [8, 32, CH], BF16).ap()
    selrows_d = nc.dram_tensor("selrows", [8, TCORE], BF16).ap()
    wm8_d = nc.dram_tensor("wm8", [1, 8], F32).ap()
    ccin_d = nc.dram_tensor("ccin", [1, 16], F32)
    ccout_d = nc.dram_tensor("ccout", [1, 16], F32)

    from contextlib import ExitStack
    stack = ExitStack()
    En = stack.enter_context
    ap = lambda h: h.ap()

    xn_fm = [ap(En(nc.sbuf_tensor(f"xnfm{h}", [P, TCORE], BF16))) for h in range(HCH)]
    acc = [ap(En(nc.sbuf_tensor(f"acc{h}", [P, QT], F32))) for h in range(HCH)]
    wstage = [ap(En(nc.sbuf_tensor(f"wstage{j}", [P, 1024], F32))) for j in range(2)]
    w_bf = [ap(En(nc.sbuf_tensor(f"wbf{m}", [P, 4096], BF16))) for m in range(3)]
    mbt = [ap(En(nc.sbuf_tensor(f"mb{j}", [P, QT], BF16))) for j in range(2)]
    wmb = ap(En(nc.sbuf_tensor("wmb", [P, 8], F32)))
    selrows_sb = [ap(En(nc.sbuf_tensor(f"selrows_sb{j}", [8, 512], BF16))) for j in range(2)]
    wsel = ap(En(nc.sbuf_tensor("wsel", [P, NT * 16], BF16)))
    scstore = ap(En(nc.sbuf_tensor("scstore", [P, NT * 8], F32)))
    selstore = ap(En(nc.sbuf_tensor("selstore", [P, NT * 8], BF16)))
    m12 = ap(En(nc.sbuf_tensor("m12", [P, NT * 2], F32)))
    den32 = ap(En(nc.sbuf_tensor("den32", [P, NT], F32)))
    rin32 = ap(En(nc.sbuf_tensor("rin32", [P, NT], F32)))
    sumx = ap(En(nc.sbuf_tensor("sumx", [P, NT], F32)))
    sumsq = ap(En(nc.sbuf_tensor("sumsq", [P, NT], F32)))
    m2c = ap(En(nc.sbuf_tensor("m2c", [P, NT], F32)))
    varc = ap(En(nc.sbuf_tensor("varc", [P, NT], F32)))
    lnvc = ap(En(nc.sbuf_tensor("lnvc", [P, NT], F32)))
    invc = ap(En(nc.sbuf_tensor("invc", [P, NT], F32)))
    nmic = ap(En(nc.sbuf_tensor("nmic", [P, NT], F32)))
    xtile = [ap(En(nc.sbuf_tensor(f"xtile{j}", [P, 1024], F32))) for j in range(3)]
    xn_tm = [ap(En(nc.sbuf_tensor(f"xntm{j}", [P, 1024], BF16))) for j in range(2)]
    obuf = [ap(En(nc.sbuf_tensor(f"obuf{j}", [P, 1024], F32))) for j in range(1)]
    ones8 = ap(En(nc.sbuf_tensor("ones8", [P, 8], F32)))
    lg2 = [ap(En(nc.sbuf_tensor(f"lg{j}", [P, 8], F32))) for j in range(2)]
    teq = [ap(En(nc.sbuf_tensor(f"teq{j}", [P, 8], F32))) for j in range(2)]
    tcum = [ap(En(nc.sbuf_tensor(f"tcum{j}", [P, 8], F32))) for j in range(2)]
    tcum2 = [ap(En(nc.sbuf_tensor(f"tcum2{j}", [P, 8], F32))) for j in range(2)]
    tf1 = [ap(En(nc.sbuf_tensor(f"tf1{j}", [P, 8], F32))) for j in range(2)]
    ts2 = [ap(En(nc.sbuf_tensor(f"ts2{j}", [P, 8], F32))) for j in range(2)]
    twn = [ap(En(nc.sbuf_tensor(f"twn{j}", [P, 8], F32))) for j in range(2)]
    dr_b = [ap(En(nc.sbuf_tensor(f"dr{j}", [64, CH], BF16))) for j in range(2)]
    bcsb = [ap(En(nc.sbuf_tensor(f"bcsb{j}", [32, CH], BF16))) for j in range(2)]
    bcast = [ap(En(nc.sbuf_tensor(f"bcast{j}", [P, CH], BF16))) for j in range(3)]
    ccast = [ap(En(nc.sbuf_tensor(f"ccast{j}", [P, CH], BF16))) for j in range(3)]
    delta_b = [ap(En(nc.sbuf_tensor(f"delta{j}", [P, CH], BF16))) for j in range(2)]
    t1_b = [ap(En(nc.sbuf_tensor(f"t1{j}", [P, CH], BF16))) for j in range(2)]
    u_b = [ap(En(nc.sbuf_tensor(f"u{j}", [P, CH], BF16))) for j in range(2)]
    da_b = [ap(En(nc.sbuf_tensor(f"da{j}", [P, CH], F32))) for j in range(2)]
    dbu_b = ap(En(nc.sbuf_tensor("dbu", [P, CH], BF16)))
    hpair = ap(En(nc.sbuf_tensor("hpair", [P, 2 * CH], F32)))
    p_b = [ap(En(nc.sbuf_tensor(f"pb{j}", [P, CH], BF16))) for j in range(2)]
    xnd_b = [ap(En(nc.sbuf_tensor(f"xnd{j}", [P, CH], BF16))) for j in range(2)]
    hstate = ap(En(nc.sbuf_tensor("hstate", [P, 128], F32)))
    g_b = [ap(En(nc.sbuf_tensor(f"gb{j}", [P, 512], BF16))) for j in range(2)]
    u2_b = [ap(En(nc.sbuf_tensor(f"u2{j}", [P, 512], BF16))) for j in range(2)]
    vtmp = ap(En(nc.sbuf_tensor("vtmp", [P, 512], BF16)))
    vtmp2 = ap(En(nc.sbuf_tensor("vtmp2", [P, 512], BF16)))
    tt_b = [ap(En(nc.sbuf_tensor(f"ttb{j}", [P, 512], BF16))) for j in range(2)]
    v_b = [ap(En(nc.sbuf_tensor(f"vb{j}", [P, 512], BF16))) for j in range(4)]
    xpw_bf = ap(En(nc.sbuf_tensor("xpwbf", [P, 768], BF16)))
    dtw_bf = ap(En(nc.sbuf_tensor("dtwbf", [64, 1024], BF16)))
    gw_bf = ap(En(nc.sbuf_tensor("gwbf", [P, 64], BF16)))
    acolx = ap(En(nc.sbuf_tensor("acolx", [P, 128], F32)))
    acole = ap(En(nc.sbuf_tensor("acole", [P, 128], F32)))
    dcol = ap(En(nc.sbuf_tensor("dcol", [P, 8], F32)))
    dtbcol = ap(En(nc.sbuf_tensor("dtbcol", [P, 8], F32)))
    lb_bc = ap(En(nc.sbuf_tensor("lbbc", [P, 8], F32)))
    identb_sb = ap(En(nc.sbuf_tensor("identbsb", [P, P], BF16)))
    identf_sb = ap(En(nc.sbuf_tensor("identfsb", [P, P], F32)))
    onesb_sb = ap(En(nc.sbuf_tensor("onesbsb", [P, 1], BF16)))
    ccsb = ap(En(nc.sbuf_tensor("ccsb", [16, 1], F32)))
    wstat2 = ap(En(nc.sbuf_tensor("wstat2", [8, 2], F32)))
    cntc = ap(En(nc.sbuf_tensor("cntc", [8, 1], F32)))
    touch = ap(En(nc.sbuf_tensor("touch", [P, 8], BF16)))
    lncnt = ap(En(nc.sbuf_tensor("lncnt", [8, 1], F32)))
    rcnt = ap(En(nc.sbuf_tensor("rcnt", [8, 1], F32)))
    wmean8 = ap(En(nc.sbuf_tensor("wmean8", [8, 1], F32)))
    epscol = ap(En(nc.sbuf_tensor("epscol", [P, 1], F32)))

    ps0 = ap(En(nc.psum_tensor("ps0", [P, 512], F32)))
    ps1 = ap(En(nc.psum_tensor("ps1", [P, 1024], BF16)))
    ps2 = ap(En(nc.psum_tensor("ps2", [P, 1024], BF16)))
    psg = ap(En(nc.psum_tensor("psg", [P, 512], F32)))
    psu = ap(En(nc.psum_tensor("psu", [P, 512], F32)))
    psd = ap(En(nc.psum_tensor("psd", [P, 512], F32)))
    psy = [ap(En(nc.psum_tensor("psy6", [P, 512], F32))),
           ap(En(nc.psum_tensor("psy7", [P, 512], F32)))]
    ps1f = ps1.bitcast(F32)
    ps2f = ps2.bitcast(F32)
    psyb = psy[1].bitcast(BF16)

    DMA_SEMS = (["mCst", "mLb", "mO", "mCC", "mWM8", "mWMB", "mSR0", "mSR1"]
                + [f"mX{j}" for j in range(3)] + [f"mStg{j}" for j in range(2)]
                + [f"mWC{m}" for m in range(3)] + [f"mWB{m}" for m in range(3)]
                + [f"mBC{j}" for j in range(3)] + [f"mMB{j}" for j in range(2)]
                + [f"mBcb{j}" for j in range(2)])
    ENG_SEMS = ["eS", "eV", "eT", "eG"]
    sems = {nm: En(nc.semaphore(nm)) for nm in DMA_SEMS + ENG_SEMS}
    cnt = {nm: 0 for nm in sems}
    step = {nm: (16 if nm in DMA_SEMS else 1) for nm in sems}
    ops = {k: [] for k in ("sync", "gpsimd", "scalar", "vector", "tensor")}

    _rc = getattr(build_program, "_racecheck", False)

    def _collect_aps(meth, args, kw):
        from concourse.ap import AP
        reads, writes = [], []
        wr_keys = ("out",)
        names = {"dma_start": (("out",), ("in_",)),
                 "matmul": (("out",), ("lhsT", "rhs")),
                 "transpose": (("out",), ("in_", "identity")),
                 "activation": (("out", "accum_out"), ("in_", "scale", "bias")),
                 "copy": (("out",), ("in_",)),
                 "tensor_copy": (("out",), ("in_",)),
                 "tensor_tensor": (("out",), ("in0", "in1")),
                 "tensor_mul": (("out", ), ("in0", "in1")),
                 "tensor_add": (("out", ), ("in0", "in1")),
                 "tensor_scalar": (("out", "accum_out"), ("in0", "scalar1", "scalar2")),
                 "tensor_scalar_add": (("out",), ("in0", "scalar1")),
                 "tensor_scalar_mul": (("out",), ("in0", "scalar1")),
                 "tensor_scalar_max": (("out",), ("in0", "scalar1")),
                 "scalar_tensor_tensor": (("out",), ("in0", "scalar", "in1")),
                 "tensor_tensor_scan": (("out",), ("data0", "data1", "initial")),
                 "reduce_max": (("out",), ("in_",)),
                 "memset": (("out",), ()),
                 "collective_compute": ((), ()),
                 }
        wk, rk = names.get(meth, ((), ()))
        allv = list(args) + [kw.get(k) for k in ("out", "in_", "in0", "in1",
                                                 "lhsT", "rhs", "identity",
                                                 "scalar1", "scalar2", "scalar",
                                                 "data0", "data1", "initial",
                                                 "accum_out", "scale", "bias")]
        def add(lst, v):
            if isinstance(v, AP):
                r = _ap_range(v)
                if r:
                    lst.append(r)
        for k in wk:
            add(writes, kw.get(k))
        for k in rk:
            add(reads, kw.get(k))
        if args:
            add(writes, args[0])
            for v in args[1:]:
                add(reads, v)
        return reads, writes

    def em(eng, meth, *args, inc=None, **kw):
        tick = None
        if inc is not None:
            cnt[inc] += step[inc]
            tick = (inc, cnt[inc])
            h, v = sems[inc], step[inc]

            def f(e, meth=meth, args=args, kw=kw, h=h, v=v):
                getattr(e, meth)(*args, **kw).then_inc(h, v)
        else:
            def f(e, meth=meth, args=args, kw=kw):
                getattr(e, meth)(*args, **kw)
        ops[eng].append(f)
        if _rc:
            r, w = _collect_aps(meth, args, kw)
            RACECHECK.append(("op", eng, meth, r, w,
                              None if tick is None else tick))
        return tick

    def wait(eng, guard):
        if guard is None:
            return
        nm, v = guard
        if v <= 0:
            return
        ops[eng].append(lambda e, s=sems[nm], v=v: e.wait_ge(s, v))
        if _rc:
            RACECHECK.append(("wait", eng, nm, v))

    def gmax(*guards):
        out = {}
        for g in guards:
            if g is None:
                continue
            nm, v = g
            out[nm] = max(out.get(nm, 0), v)
        return [(nm, v) for nm, v in out.items()]

    def waits(eng, *guards):
        for g in gmax(*guards):
            wait(eng, g)

    for (dst, src) in ((wstage[0][:, 0:768], xpw_d),
                       (wstage[1][0:64, 0:1024], dtw_d),
                       (wstage[0][:, 768:832], gw_d),
                       (acole, alog_d),
                       (dcol, dvec_d),
                       (dtbcol, dtb_d),
                       (identb_sb, identb_d),
                       (identf_sb, identf_d),
                       (onesb_sb, onesb_d)):
        t_const = em("sync", "dma_start", inc="mCst", out=dst, in_=src)
    g_lb = em("gpsimd", "dma_start", inc="mLb", out=lb_bc, in_=lb_d.partition_broadcast(P))
    wait("scalar", t_const)
    em("scalar", "copy", out=xpw_bf, in_=wstage[0][:, 0:768])
    em("scalar", "copy", out=dtw_bf, in_=wstage[1][0:64, 0:1024])
    em("scalar", "copy", out=gw_bf, in_=wstage[0][:, 768:832])
    t_sc = em("scalar", "activation", inc="eS", out=acolx, in_=acole, func=AF.Exp)
    wait("vector", t_sc)
    em("vector", "tensor_scalar_mul", acole, acolx, -1.0)
    em("vector", "memset", epscol, 1e-5)
    em("vector", "memset", ones8, 1.0, inc="eV")
    acol = acole
    consts_eS = t_sc
    consts_eV = ("eV", cnt["eV"])

    cast_st = {"stage": [consts_eS, consts_eS], "wcw": [None, None, None]}
    cast_final = [None, None, None]

    def cast_gen():
        for pid in PASS_ORDER:
            for m in range(3):
                for hf in range(4):
                    si = (4 * m + hf) % 2
                    waits("sync", cast_st["stage"][si])
                    mst = em("sync", "dma_start", inc=f"mStg{si}", out=wstage[si],
                             in_=ws_d[pid, m, :, hf * 1024:(hf + 1) * 1024])
                    waits("scalar", mst, cast_st["wcw"][m])
                    cast_st["stage"][si] = em("scalar", "copy", inc="eS",
                                              out=w_bf[m][:, hf * 1024:(hf + 1) * 1024],
                                              in_=wstage[si])
                    yield
                wl = em("scalar", "copy", inc="eS", out=touch[:, m:m + 1],
                        in_=w_bf[m][:, 4095:4096])
                wait("gpsimd", wl)
                wcw = em("gpsimd", "dma_start", inc=f"mWC{m}",
                         out=wcache_d[pid, m], in_=w_bf[m])
                cast_st["wcw"][m] = wcw
                cast_final[m] = wcw
                yield

    wc_final = [None, None, None]

    xload = [None] * NT
    s3t = [None] * NT
    v1t = [None] * NT
    s5t = [None] * NT
    v2t = [None] * NT
    s6t = [None] * NT
    t1t = [None] * NT
    evt = [None] * NT

    def p0a_load(k):
        if k >= 3:
            wait("sync", s6t[k - 3])
        xload[k] = em("sync", "dma_start", inc=f"mX{k % 3}", out=xtile[k % 3],
                      in_=x_d[k * P:(k + 1) * P, :])

    def p0a_A(k):
        wait("scalar", xload[k])
        em("scalar", "activation", out=obuf[0], in_=xtile[k % 3], func=AF.Square,
           accum_out=sumsq[:, k:k + 1])
        em("scalar", "activation", out=obuf[0], in_=xtile[k % 3], func=AF.Copy,
           scale=float(1.0 / 1024.0), accum_out=sumx[:, k:k + 1])
        s3t[k] = em("scalar", "activation", inc="eS", out=m2c[:, k:k + 1],
                    in_=sumx[:, k:k + 1], func=AF.Square)
        wait("vector", s3t[k])
        v1t[k] = em("vector", "tensor_scalar", inc="eV", out=varc[:, k:k + 1],
                    in0=sumsq[:, k:k + 1], scalar1=float(1.0 / 1024.0),
                    scalar2=m2c[:, k:k + 1], op0=ALU.mult, op1=ALU.subtract)

    def p0a_B(k):
        wait("scalar", v1t[k])
        waits("scalar", consts_eV)
        em("scalar", "activation", out=lnvc[:, k:k + 1], in_=varc[:, k:k + 1],
           func=AF.Ln, bias=epscol)
        s5t[k] = em("scalar", "activation", inc="eS", out=invc[:, k:k + 1],
                    in_=lnvc[:, k:k + 1], func=AF.Exp, scale=-0.5)

    def p0a_C(k):
        waits("vector", s5t[k], evt[k - 2] if k >= 2 else None)
        s6t[k] = em("vector", "tensor_scalar", inc="eV", out=xn_tm[k % 2],
                    in0=xtile[k % 3], scalar1=sumx[:, k:k + 1],
                    scalar2=invc[:, k:k + 1], op0=ALU.subtract, op1=ALU.mult)
        pst = ps1 if (k % 2 == 0) else ps2
        waits("tensor", s6t[k], evt[k - 2] if k >= 2 else None)
        for h in range(HCH):
            last = (h == HCH - 1)
            t1t[k] = em("tensor", "transpose", inc="eT" if last else None,
                        out=pst[:, h * 128:(h + 1) * 128],
                        in_=xn_tm[k % 2][:, h * 128:(h + 1) * 128],
                        identity=identb_sb) or t1t[k]
        wait("vector", t1t[k])
        for h in range(HCH):
            last = (h == HCH - 1)
            evt[k] = em("vector", "tensor_copy", inc="eV" if last else None,
                        out=xn_fm[h][:, k * 128:(k + 1) * 128],
                        in_=pst[:, h * 128:(h + 1) * 128]) or evt[k]

    cg = cast_gen()
    cg_alive = True
    for k in range(NT + 2):
        if k < NT:
            p0a_load(k)
            p0a_A(k)
        if 1 <= k <= NT:
            p0a_B(k - 1)
        if k >= 2:
            p0a_C(k - 2)
        if cg_alive:
            for _ in range(5):
                try:
                    next(cg)
                except StopIteration:
                    cg_alive = False
                    break
    while cg_alive:
        try:
            next(cg)
        except StopIteration:
            cg_alive = False
    for m in range(3):
        wc_final[m] = (f"mWC{m}", cnt[f"mWC{m}"])
    p0a_S_done = ("eS", cnt["eS"])
    p0a_V_done = ("eV", cnt["eV"])

    lgt = [None] * NT
    sct = [None] * NT
    tkt = [None] * NT
    selevt = [None] * (NT // 4)
    selrow_dma = [None] * (NT // 4)

    def gate_mm(k):
        psb = ps1f if (k % 2 == 0) else ps2f
        waits("tensor", evt[NT - 1] if k < 2 else lgt[k - 2], evt[k], consts_eS)
        t = None
        for h in range(HCH):
            last = (h == HCH - 1)
            t = em("tensor", "matmul", inc="eT" if last else None,
                   out=psb[:, 0:8], lhsT=xn_fm[h][:, k * 128:(k + 1) * 128],
                   rhs=gw_bf[:, h * 8:(h + 1) * 8],
                   start=(h == 0), stop=last) or t
        return t

    def topk_seq(k, j):
        sc = scstore[:, k * 8:(k + 1) * 8]
        sel = selstore[:, k * 8:(k + 1) * 8]
        return [
            ("reduce_max", dict(out=m12[:, 2 * k:2 * k + 1], in_=sc, axis=AX.X)),
            ("tensor_scalar", dict(out=teq[j], in0=sc, scalar1=m12[:, 2 * k:2 * k + 1],
                                   scalar2=None, op0=ALU.is_equal)),
            ("tensor_tensor_scan", dict(out=tcum[j], data0=ones8, data1=teq[j],
                                        initial=0.0, op0=ALU.mult, op1=ALU.add)),
            ("tensor_scalar", dict(out=tcum2[j], in0=tcum[j], scalar1=1.0,
                                   scalar2=None, op0=ALU.is_equal)),
            ("tensor_mul", dict(out=tf1[j], in0=teq[j], in1=tcum2[j])),
            ("scalar_tensor_tensor", dict(out=ts2[j], in0=tf1[j], scalar=-1e30,
                                          in1=sc, op0=ALU.mult, op1=ALU.add)),
            ("reduce_max", dict(out=m12[:, 2 * k + 1:2 * k + 2], in_=ts2[j], axis=AX.X)),
            ("tensor_scalar", dict(out=teq[j], in0=ts2[j],
                                   scalar1=m12[:, 2 * k + 1:2 * k + 2],
                                   scalar2=None, op0=ALU.is_equal)),
            ("tensor_tensor_scan", dict(out=tcum[j], data0=ones8, data1=teq[j],
                                        initial=0.0, op0=ALU.mult, op1=ALU.add)),
            ("tensor_scalar", dict(out=tcum2[j], in0=tcum[j], scalar1=1.0,
                                   scalar2=None, op0=ALU.is_equal)),
            ("tensor_mul", dict(out=twn[j], in0=teq[j], in1=tcum2[j])),
            ("tensor_add", dict(out=sel, in0=tf1[j], in1=twn[j])),
            ("tensor_copy", dict(out=wsel[:, k * 16 + 8:k * 16 + 16], in_=sel)),
        ]

    for pr in range(NT // 2):
        a, b = 2 * pr, 2 * pr + 1
        ta = gate_mm(a)
        tb = gate_mm(b)
        waits("vector", ta, g_lb, consts_eV)
        lgt[a] = em("vector", "tensor_add", inc="eV", out=lg2[0], in0=ps1f[:, 0:8], in1=lb_bc)
        wait("vector", tb)
        lgt[b] = em("vector", "tensor_add", inc="eV", out=lg2[1], in0=ps2f[:, 0:8], in1=lb_bc)
        wait("scalar", lgt[a])
        em("scalar", "activation", out=scstore[:, a * 8:(a + 1) * 8], in_=lg2[0], func=AF.Sigmoid)
        wait("scalar", lgt[b])
        sct[a] = sct[b] = em("scalar", "activation", inc="eS",
                             out=scstore[:, b * 8:(b + 1) * 8], in_=lg2[1], func=AF.Sigmoid)
        wait("vector", sct[a])
        qa, qb = topk_seq(a, 0), topk_seq(b, 1)
        for i, ((ma_, kwa), (mb_, kwb)) in enumerate(zip(qa, qb)):
            last = (i == len(qa) - 1)
            em("vector", ma_, **kwa)
            tkt[a] = tkt[b] = em("vector", mb_, inc="eV" if last else None, **kwb) or tkt[a]
        for k in (a, b):
            g4 = k // 4
            waits("tensor", tkt[k], selevt[g4 - 1] if (k % 4 == 0 and g4 >= 1) else None)
            tt = em("tensor", "transpose", inc="eT",
                    out=psyb[0:8, (k % 4) * 128:(k % 4 + 1) * 128],
                    in_=wsel[:, k * 16 + 8:k * 16 + 16], identity=identb_sb)
            if k % 4 == 3:
                waits("scalar", tt, selevt[g4 - 2] and None)
                if g4 >= 2:
                    wait("scalar", selrow_dma[g4 - 2])
                selevt[g4] = em("scalar", "copy", inc="eS",
                                out=selrows_sb[g4 % 2],
                                in_=psyb[0:8, 0:512])
                wait("gpsimd", selevt[g4])
                selrow_dma[g4] = em("gpsimd", "dma_start", inc=f"mSR{g4 % 2}",
                   out=selrows_d[:, g4 * 512:(g4 + 1) * 512], in_=selrows_sb[g4 % 2])

    vden = em("vector", "scalar_tensor_tensor", inc="eV", out=den32,
              in0=m12[:, 0:64:2], scalar=1e-8, in1=m12[:, 1:64:2],
              op0=ALU.add, op1=ALU.add)
    wait("scalar", vden)
    em("scalar", "activation", out=lnvc, in_=den32, func=AF.Ln)
    srin = em("scalar", "activation", inc="eS", out=rin32, in_=lnvc, func=AF.Exp,
              scale=-1.0)
    wait("vector", srin)
    wt = [None] * NT
    for pr in range(NT // 2):
        a, b = 2 * pr, 2 * pr + 1
        for j, k in ((0, a), (1, b)):
            em("vector", "tensor_mul", out=twn[j], in0=selstore[:, k * 8:(k + 1) * 8],
               in1=scstore[:, k * 8:(k + 1) * 8])
        for j, k in ((0, a), (1, b)):
            wt[k] = em("vector", "tensor_scalar", inc="eV" if k == b else None,
                       out=wsel[:, k * 16:k * 16 + 8], in0=twn[j],
                       scalar1=rin32[:, k:k + 1], scalar2=None, op0=ALU.mult)
        wt[a] = wt[b]
    for k in range(NT):
        if k == 0:
            wait("tensor", wt[NT - 1])
        tpart = em("tensor", "matmul", inc="eT" if k == NT - 1 else None,
                   out=psy[0][0:16, 0:1], lhsT=wsel[:, k * 16:(k + 1) * 16],
                   rhs=onesb_sb, start=(k == 0), stop=(k == NT - 1))
    wait("scalar", tpart)
    spart = em("scalar", "copy", inc="eS", out=ccsb, in_=psy[0][0:16, 0:1])
    wait("gpsimd", spart)
    em("gpsimd", "dma_start", inc="mCC", out=ccin_d.ap().rearrange("1 p -> p 1"), in_=ccsb)
    wait("gpsimd", ("mCC", cnt["mCC"]))
    gcc = em("gpsimd", "collective_compute", "AllReduce", ALU.add, inc="eG",
             replica_groups=[list(range(8))], ins=[ccin_d.ap().opt()],
             outs=[ccout_d.ap().opt()])
    wait("gpsimd", gcc)
    em("gpsimd", "dma_start", inc="mCC", out=wstat2[:, 0:1],
       in_=ccout_d.ap()[0:1, 0:8].rearrange("1 p -> p 1"))
    mws = em("gpsimd", "dma_start", inc="mCC", out=wstat2[:, 1:2],
             in_=ccout_d.ap()[0:1, 8:16].rearrange("1 p -> p 1"))
    wait("vector", mws)
    vcnt = em("vector", "tensor_scalar_max", cntc, wstat2[:, 1:2], 1.0, inc="eV")
    wait("scalar", vcnt)
    em("scalar", "activation", out=lncnt, in_=cntc, func=AF.Ln)
    srcnt = em("scalar", "activation", inc="eS", out=rcnt, in_=lncnt, func=AF.Exp,
               scale=-1.0)
    wait("vector", srcnt)
    vwm = em("vector", "scalar_tensor_tensor", inc="eV", out=wmean8,
             in0=wstat2[:, 0:1], scalar=0.5, in1=rcnt, op0=ALU.mult, op1=ALU.mult)
    wait("gpsimd", vwm)
    em("gpsimd", "dma_start", inc="mWM8", out=wm8_d.rearrange("1 p -> p 1"), in_=wmean8)
    wait("gpsimd", ("mWM8", cnt["mWM8"]))
    for e in range(8):
        em("gpsimd", "dma_start", inc="mWMB", out=wmb[:, e:e + 1],
           in_=wm8_d[0:1, e:e + 1].partition_broadcast(P))
    g_wmb = ("mWMB", cnt["mWMB"])
    g_selrow = [("mSR0", cnt["mSR0"]), ("mSR1", cnt["mSR1"])]

    p_b.append(wsel[:, 0:512])
    p_b.append(scstore.bitcast(BF16)[:, 0:512])
    da_b.append(wstage[0][:, 0:512])
    da_b.append(wstage[0][:, 512:1024])
    st = {
        "dr": [None, None], "bcsbw": [None, None],
        "bcast": [None] * 3, "ccast": [None] * 3,
        "delta": [None, None], "u": [None, None], "da": [None] * 2,
        "p": [None] * 2, "xndT": [None, None],
        "bank0": spart, "bank6": spart, "bank7": selevt[-1],
        "pst": [tpart, tpart],
        "bank3": None, "bank3b": lgt[NT - 1], "bank3v": None, "bank4": None,
        "bank4b": lgt[NT - 1], "bank5": None,
        "g": [None, None], "v": [None] * 4, "mb": [None, None],
        "wbf_used": [("mWC0", cnt["mWC0"]), ("mWC1", cnt["mWC1"]),
                     ("mWC2", cnt["mWC2"])],
        "xt": [s6t[30], s6t[31], s6t[29]],
        "ob": [p0a_S_done],
        "b12": None, "epi_t": None,
    }

    def mamba_gen(q):
        for ci in range(2):
            yield from mamba_chunk(q, ci)

    def mamba_chunk(q, ci):
        c = 2 * q + ci
        tok0 = c * CH
        ji = c % 2
        waits("tensor", st["bank0"], p0a_V_done)
        txp = None
        for h in range(HCH):
            last = (h == HCH - 1)
            txp = em("tensor", "matmul", inc="eT" if last else None,
                     out=ps0[0:96, :], lhsT=xpw_bf[:, h * 96:(h + 1) * 96],
                     rhs=xn_fm[h][:, tok0:tok0 + CH],
                     start=(h == 0), stop=last) or txp
        waits("scalar", txp, st["dr"][ji], st["bcsbw"][ji])
        em("scalar", "copy", out=dr_b[ji], in_=ps0[0:64, :])
        sdr = em("scalar", "copy", inc="eS", out=bcsb[ji], in_=ps0[64:96, :])
        st["bank0"] = sdr
        wait("gpsimd", sdr)
        gw_bc = em("gpsimd", "dma_start", inc=f"mBcb{ji}", out=bcb_d[c % 8], in_=bcsb[ji])
        st["bcsbw"][ji] = gw_bc
        yield
        for pair in range(4):
            dAc, dBc = 2 * pair, 2 * pair + 1
            for di, dch in ((0, dAc), (1, dBc)):
                waits("tensor", st["bank0"], sdr)
                td = em("tensor", "matmul", inc="eT", out=ps0[:, :],
                        lhsT=dtw_bf[:, dch * 128:(dch + 1) * 128], rhs=dr_b[ji],
                        start=True, stop=True)
                st["dr"][ji] = td
                waits("scalar", td, st["u"][di])
                em("scalar", "activation", out=t1_b[di], in_=ps0[:, :], func=AF.Exp,
                   bias=dtbcol[:, dch:dch + 1])
                sdelta = em("scalar", "activation", inc="eS", out=delta_b[di],
                            in_=t1_b[di], func=AF.Ln, bias=1.0)
                st["bank0"] = sdelta
                wait("vector", sdelta)
                st["u"][di] = em("vector", "tensor_mul", inc="eV", out=u_b[di],
                                 in0=delta_b[di], in1=xn_fm[dch][:, tok0:tok0 + CH])
                waits("scalar", st["xndT"][di])
                sxnd = em("scalar", "activation", inc="eS", out=xnd_b[di],
                          in_=xn_fm[dch][:, tok0:tok0 + CH], func=AF.Copy,
                          scale=dcol[:, dch:dch + 1])
                st["xnd"] = sxnd
                waits("tensor", sxnd, st["bank6"] if di == 0 else st["bank7"])
                st["xndT"][di] = em("tensor", "matmul", inc="eT", out=psy[di],
                                    lhsT=identb_sb, rhs=xnd_b[di],
                                    start=True, stop=False)
            yield
            svn = [None] * (N16 + 1)
            stn = [None] * (N16 + 1)
            ssn = [None] * (N16 + 1)

            def emit_da(n):
                waits("scalar", svn[n - 2] if n >= 2 else st["da"][0])
                em("scalar", "activation", out=da_b[(n % 2) * 2],
                   in_=delta_b[0], func=AF.Exp,
                   scale=acol[:, dAc * 16 + n:dAc * 16 + n + 1])
                ssn[n] = em("scalar", "activation", inc="eS",
                            out=da_b[(n % 2) * 2 + 1], in_=delta_b[1], func=AF.Exp,
                            scale=acol[:, dBc * 16 + n:dBc * 16 + n + 1])

            emit_da(0)
            for n in range(N16):
                bi = n % 3
                waits("gpsimd", gw_bc, st["bcast"][bi])
                em("gpsimd", "dma_start", inc=f"mBC{bi}", out=bcast[bi],
                   in_=bcb_d[c % 8][n:n + 1, :].partition_broadcast(P))
                mbc = em("gpsimd", "dma_start", inc=f"mBC{bi}", out=ccast[bi],
                         in_=bcb_d[c % 8][16 + n:17 + n, :].partition_broadcast(P))
                if n + 1 < N16:
                    emit_da(n + 1)
                waits("vector", ssn[n],
                      stn[n - 2] if n >= 2 else st["pst"][n])
                waits("vector", mbc)
                for di in (0, 1):
                    em("vector", "tensor_mul", out=dbu_b, in0=u_b[di], in1=bcast[bi])
                    em("vector", "tensor_tensor_scan",
                       out=hpair[:, di * CH:(di + 1) * CH],
                       data0=da_b[(n % 2) * 2 + di], data1=dbu_b,
                       initial=hstate[:, (dAc + di) * 16 + n:(dAc + di) * 16 + n + 1],
                       op0=ALU.mult, op1=ALU.add)
                    em("vector", "tensor_mul", out=p_b[(2 * n + di) % 4],
                       in0=hpair[:, di * CH:(di + 1) * CH], in1=ccast[bi])
                svn[n] = em("vector", "tensor_copy", inc="eV",
                            out=hstate[:, 32 * pair + n:32 * pair + n + 17:16],
                            in_=hpair[:, CH - 1:2 * CH:CH])
                st["bcast"][bi] = svn[n]
                if n == N16 - 1:
                    st["da"][0] = svn[N16 - 2]
                wait("tensor", svn[n])
                em("tensor", "matmul", out=psy[0], lhsT=identb_sb,
                   rhs=p_b[(2 * n) % 4], start=False, stop=(n == N16 - 1))
                stn[n] = em("tensor", "matmul", inc="eT", out=psy[1],
                            lhsT=identb_sb, rhs=p_b[(2 * n + 1) % 4],
                            start=False, stop=(n == N16 - 1))
                yield
            st["pst"] = [stn[N16 - 2], stn[N16 - 1]]
            for di, dch in ((0, dAc), (1, dBc)):
                wait("vector", stn[N16 - 1])
                vacc = em("vector", "tensor_tensor", inc="eV",
                          out=acc[dch][:, ci * CH:(ci + 1) * CH],
                          in0=acc[dch][:, ci * CH:(ci + 1) * CH], in1=psy[di],
                          op=ALU.add)
                if di == 0:
                    st["bank6"] = vacc
                else:
                    st["bank7"] = vacc

    def ffn_gen(q):
        for pi, pid in enumerate(PASS_ORDER):
            yield from ffn_pass(q, pi, pid)

    def ffn_pass(q, pi, pid):
        masked = pid < 8
        wload = [None, None, None]
        for m in range(3):
            waits("sync", st["wbf_used"][m], wc_final[m])
            wload[m] = em("sync", "dma_start", inc=f"mWB{m}", out=w_bf[m],
                          in_=wcache_d[pid, m])
        yield
        if masked:
            mi = pi % 2
            waits("gpsimd", g_selrow[0], g_selrow[1], st["mb"][mi])
            mmb = em("gpsimd", "dma_start", inc=f"mMB{mi}", out=mbt[mi],
                     in_=selrows_d[pid:pid + 1, q * QT:(q + 1) * QT].partition_broadcast(P))
            waits("scalar", g_wmb)
        for ti in range(2):
            tok0 = q * QT + ti * 512
            for ich in range(4):
                gi = ich % 2
                gpsum = psg if gi == 0 else ps1f
                upsum = psu if gi == 0 else ps2f
                bk3 = "bank3" if gi == 0 else "bank3b"
                bk4 = "bank4" if gi == 0 else "bank4b"
                waits("tensor", wload[0], wload[1], st[bk3],
                      st["bank3v"] if gi == 0 else st["b12"], p0a_V_done)
                tg = None
                for h in range(HCH):
                    last = (h == HCH - 1)
                    tg = em("tensor", "matmul", inc="eT" if last else None, out=gpsum,
                            lhsT=w_bf[0][:, h * 512 + ich * 128:h * 512 + (ich + 1) * 128],
                            rhs=xn_fm[h][:, tok0:tok0 + 512],
                            start=(h == 0), stop=last) or tg
                waits("tensor", st[bk4])
                tu = None
                for h in range(HCH):
                    last = (h == HCH - 1)
                    tu = em("tensor", "matmul", inc="eT" if last else None, out=upsum,
                            lhsT=w_bf[1][:, h * 512 + ich * 128:h * 512 + (ich + 1) * 128],
                            rhs=xn_fm[h][:, tok0:tok0 + 512],
                            start=(h == 0), stop=last) or tu
                waits("scalar", tg, st["g"][gi])
                em("scalar", "copy", out=g_b[gi], in_=gpsum)
                st[bk3] = em("scalar", "activation", inc="eS", out=tt_b[gi],
                             in_=gpsum, func=AF.Tanh, scale=0.5)
                waits("scalar", tu)
                if masked:
                    st[bk4] = em("scalar", "activation", inc="eS", out=u2_b[gi],
                                 in_=upsum, func=AF.Copy, scale=wmb[:, pid:pid + 1])
                else:
                    st[bk4] = em("scalar", "activation", inc="eS", out=u2_b[gi],
                                 in_=upsum, func=AF.Copy, scale=0.5)
                waits("vector", st[bk4], st["v"][ich],
                      mmb if (masked and ti == 0 and ich == 0) else None)
                em("vector", "tensor_mul", out=vtmp, in0=g_b[gi], in1=u2_b[gi])
                if masked:
                    em("vector", "scalar_tensor_tensor", out=vtmp2, in0=tt_b[gi],
                       scalar=1.0, in1=mbt[pi % 2][:, ti * 512:(ti + 1) * 512],
                       op0=ALU.add, op1=ALU.mult)
                    vv = em("vector", "tensor_mul", inc="eV", out=v_b[ich],
                            in0=vtmp, in1=vtmp2)
                else:
                    em("vector", "tensor_scalar_add", vtmp2, tt_b[gi], 1.0)
                    vv = em("vector", "tensor_mul", inc="eV", out=v_b[ich],
                            in0=vtmp, in1=vtmp2)
                st["g"][gi] = vv
                if masked and ti == 1 and ich == 3:
                    st["mb"][pi % 2] = vv
                if ich == 3 and ti == 1:
                    st["wbf_used"][0] = ("eT", cnt["eT"])
                    st["wbf_used"][1] = ("eT", cnt["eT"])
                if ich == 3:
                    vlast_t = vv
                yield
            vlast = vlast_t
            for h in range(HCH):
                bank = psd if (h % 2 == 0) else psg
                bkey = "bank5" if (h % 2 == 0) else "bank3v"
                waits("tensor", vlast, st[bkey], wload[2],
                      st["bank3"] if h % 2 == 1 else None)
                tdn = None
                for ich in range(4):
                    last = (ich == 3)
                    tdn = em("tensor", "matmul", inc="eT" if last else None, out=bank,
                             lhsT=w_bf[2][:, ich * 1024 + h * 128:ich * 1024 + (h + 1) * 128],
                             rhs=v_b[ich], start=(ich == 0), stop=last) or tdn
                if h == HCH - 1:
                    for ich in range(4):
                        st["v"][ich] = tdn
                    if ti == 1:
                        st["wbf_used"][2] = tdn
                wait("vector", tdn)
                st[bkey] = em("vector", "tensor_tensor", inc="eV",
                              out=acc[h][:, ti * 512:(ti + 1) * 512],
                              in0=acc[h][:, ti * 512:(ti + 1) * 512], in1=bank,
                              op=ALU.add)
                if h % 2 == 1:
                    yield

    def epilogue(q, acc_done):
        for tb in range(8):
            gt = q * 8 + tb
            xi = tb % 3
            waits("sync", st["xt"][xi])
            mxr = em("sync", "dma_start", inc=f"mX{xi}", out=xtile[xi],
                     in_=x_d[gt * P:(gt + 1) * P, :])
            waits("tensor", acc_done, st["b12"], st["bank3b"], st["bank4b"])
            ttr = None
            for h in range(HCH):
                psb = ps1f if h < 4 else ps2f
                last = (h == HCH - 1)
                ttr = em("tensor", "transpose", inc="eT" if last else None,
                         out=psb[:, (h % 4) * 128:(h % 4 + 1) * 128],
                         in_=acc[h][:, tb * 128:(tb + 1) * 128],
                         identity=identf_sb) or ttr
            oi = 0
            waits("vector", ttr, mxr, st["ob"][oi])
            em("vector", "tensor_tensor", out=obuf[oi][:, 0:512], in0=ps1f,
               in1=xtile[xi][:, 0:512], op=ALU.add)
            vep = em("vector", "tensor_tensor", inc="eV", out=obuf[oi][:, 512:1024],
                     in0=ps2f, in1=xtile[xi][:, 512:1024], op=ALU.add)
            st["b12"] = vep
            st["xt"][xi] = vep
            wait("sync", vep)
            em("sync", "dma_start", inc="mO", out=out_d[gt * P:(gt + 1) * P, :],
               in_=obuf[oi])
            st["ob"][oi] = ("mO", cnt["mO"])

    for q in range(NQ):
        for h in range(HCH):
            waits("vector", st["epi_t"])
            em("vector", "memset", acc[h], 0.0)
        if q % 2 == 0:
            em("vector", "memset", hstate, 0.0)
        em("vector", "nop", inc="eV")
        INTERLEAVE = globals().get("_INTERLEAVE", True)
        gm = mamba_gen(q)
        gf = ffn_gen(q)
        alive_m, alive_f = True, True
        while alive_m or alive_f:
            if alive_f:
                for _ in range(2 if INTERLEAVE else 10 ** 9):
                    try:
                        next(gf)
                    except StopIteration:
                        alive_f = False
                        break
            if alive_m:
                for _ in range(1 if INTERLEAVE else 10 ** 9):
                    try:
                        next(gm)
                    except StopIteration:
                        alive_m = False
                        break
        acc_done = ("eV", cnt["eV"])
        epilogue(q, acc_done)
        st["epi_t"] = ("eT", cnt["eT"])

    wait("gpsimd", ("mO", cnt["mO"]))

    block = En(nc.Block())

    @block.sync
    def _(e):
        for f in ops["sync"]:
            f(e)

    @block.gpsimd
    def _(e):
        for f in ops["gpsimd"]:
            f(e)

    @block.scalar
    def _(e):
        for f in ops["scalar"]:
            f(e)

    @block.vector
    def _(e):
        for f in ops["vector"]:
            f(e)

    @block.tensor
    def _(e):
        for f in ops["tensor"]:
            f(e)

    stack.pop_all()
    return nc


def _host_prep(inputs):
    f32 = np.float32
    gw8 = np.asarray(inputs["L0_ffn_expert_gate_w"], f32)
    uw8 = np.asarray(inputs["L0_ffn_expert_up_w"], f32)
    dw8 = np.asarray(inputs["L0_ffn_expert_down_w"], f32)
    sg = np.asarray(inputs["L0_ffn_shared_gate_w"], f32)
    su = np.asarray(inputs["L0_ffn_shared_up_w"], f32)
    sd = np.asarray(inputs["L0_ffn_shared_down_w"], f32)

    def packT(w):
        wt = np.ascontiguousarray(w.T)
        I, O = wt.shape
        return np.ascontiguousarray(
            wt.reshape(I // 128, 128, O).transpose(1, 0, 2).reshape(128, (I // 128) * O))

    wstream = np.empty((NPASS, 3, P, 4096), f32)
    for e in range(8):
        wstream[e, 0] = packT(gw8[e])
        wstream[e, 1] = packT(uw8[e])
        wstream[e, 2] = packT(dw8[e])
    wstream[8, 0] = packT(sg[0:512])
    wstream[8, 1] = packT(su[0:512])
    wstream[8, 2] = packT(sd[:, 0:512])
    wstream[9, 0] = packT(sg[512:1024])
    wstream[9, 1] = packT(su[512:1024])
    wstream[9, 2] = packT(sd[:, 512:1024])

    xpw = packT(np.asarray(inputs["L0_attn_x_proj_w"], f32))
    dtw = np.ascontiguousarray(np.asarray(inputs["L0_attn_dt_proj_w"], f32).T)
    alog = np.ascontiguousarray(
        np.asarray(inputs["L0_attn_A_log"], f32).reshape(8, 128, 16)
        .transpose(1, 0, 2).reshape(128, 128))
    dvec = np.ascontiguousarray(
        np.asarray(inputs["L0_attn_D"], f32).reshape(8, 128).T)
    dtb = np.ascontiguousarray(
        np.asarray(inputs["L0_attn_dt_proj_b"], f32).reshape(8, 128).T)
    gwm = packT(np.asarray(inputs["L0_ffn_gate_w"], f32))
    lb = np.asarray(inputs["L0_ffn_lb_bias"], f32).reshape(1, 8)
    identb = np.eye(P, dtype=BF)
    identf = np.eye(P, dtype=np.float32)
    onesb = np.ones((P, 1), dtype=BF)
    return dict(wstream=wstream, xpw=xpw, dtw=dtw, alog=alog, dvec=dvec, dtb=dtb,
                gw=gwm, lb=lb, identb=identb, identf=identf, onesb=onesb)


def kernel(**inputs):
    x = np.asarray(inputs["L0_attn_x"], np.float32)
    shared = _host_prep(inputs)
    nc = build_program()
    in_maps = []
    for c in range(8):
        m = dict(shared)
        m["x"] = np.ascontiguousarray(x[2 * c:2 * c + 2].reshape(TCORE, 1024))
        in_maps.append(m)
    res = run_bass_kernel_spmd(nc, in_maps, core_ids=list(range(8)))
    out = np.concatenate([r["out"].reshape(2, 2048, 1024) for r in res.results], axis=0)
    return out.astype(np.float32)


# version 19
# speedup vs baseline: 1.1000x; 1.1000x over previous
import numpy as np
import ml_dtypes
import concourse.bass as bass
import concourse.mybir as mybir
from concourse.bass_utils import run_bass_kernel_spmd

F32 = mybir.dt.float32
BF16 = mybir.dt.bfloat16
AF = mybir.ActivationFunctionType
ALU = mybir.AluOpType
AX = mybir.AxisListType

P = 128
HCH = 8
TCORE = 4096
NT = 32
NQ = 4
QT = 1024
CH = 512
N16 = 16
NPASS = 10
PASS_ORDER = [8, 9, 0, 1, 2, 3, 4, 5, 6, 7]
BF = ml_dtypes.bfloat16
RACECHECK = []


def _ap_range(a):
    try:
        t = a.tensor
        name = t.name
        import concourse.mybir as _mb
        esz = _mb.dt.size(a.dtype)
        off = a.offset if isinstance(a.offset, int) else 0
        dims = [(int(s), int(c)) for s, c in list(a.ap)]
        if not dims:
            return (name, str(a.space), 0, 128, off * esz, off * esz + esz)
        rowstep, pcnt = dims[0]
        if rowstep <= 0:
            rowstep = 1
        plo = off // rowstep
        flo = off % rowstep
        ext = 0
        for stp, cnt in dims[1:]:
            if cnt > 1:
                ext += abs(stp) * (cnt - 1)
        return (name, str(a.space), plo, plo + pcnt, flo * esz,
                (flo + ext + 1) * esz)
    except Exception:
        return None


def build_program():
    nc = bass.Bass("TRN2", target_bir_lowering=False, debug=False)

    x_d = nc.dram_tensor("x", [TCORE, 1024], F32, kind="ExternalInput").ap()
    ws_d = nc.dram_tensor("wstream", [NPASS, 3, P, 4096], F32, kind="ExternalInput").ap()
    xpw_d = nc.dram_tensor("xpw", [P, 768], F32, kind="ExternalInput").ap()
    dtw_d = nc.dram_tensor("dtw", [64, 1024], F32, kind="ExternalInput").ap()
    alog_d = nc.dram_tensor("alog", [P, 128], F32, kind="ExternalInput").ap()
    dvec_d = nc.dram_tensor("dvec", [P, 8], F32, kind="ExternalInput").ap()
    dtb_d = nc.dram_tensor("dtb", [P, 8], F32, kind="ExternalInput").ap()
    gw_d = nc.dram_tensor("gw", [P, 64], F32, kind="ExternalInput").ap()
    lb_d = nc.dram_tensor("lb", [1, 8], F32, kind="ExternalInput").ap()
    identb_d = nc.dram_tensor("identb", [P, P], BF16, kind="ExternalInput").ap()
    identf_d = nc.dram_tensor("identf", [P, P], F32, kind="ExternalInput").ap()
    onesb_d = nc.dram_tensor("onesb", [P, 1], BF16, kind="ExternalInput").ap()
    out_d = nc.dram_tensor("out", [TCORE, 1024], F32, kind="ExternalOutput").ap()
    wcache_d = nc.dram_tensor("wcache", [NPASS, 3, P, 4096], BF16).ap()
    bcb_d = nc.dram_tensor("bcb", [8, 32, CH], BF16).ap()
    selrows_d = nc.dram_tensor("selrows", [8, TCORE], BF16).ap()
    wm8_d = nc.dram_tensor("wm8", [1, 8], F32).ap()
    ccin_d = nc.dram_tensor("ccin", [1, 16], F32)
    ccout_d = nc.dram_tensor("ccout", [1, 16], F32)

    from contextlib import ExitStack
    stack = ExitStack()
    En = stack.enter_context
    ap = lambda h: h.ap()

    xn_fm = [ap(En(nc.sbuf_tensor(f"xnfm{h}", [P, TCORE], BF16))) for h in range(HCH)]
    acc = [ap(En(nc.sbuf_tensor(f"acc{h}", [P, QT], F32))) for h in range(HCH)]
    wstage = [ap(En(nc.sbuf_tensor(f"wstage{j}", [P, 1024], F32))) for j in range(2)]
    w_bf = [ap(En(nc.sbuf_tensor(f"wbf{m}", [P, 4096], BF16))) for m in range(3)]
    mbt = [ap(En(nc.sbuf_tensor(f"mb{j}", [P, QT], BF16))) for j in range(2)]
    wmb = ap(En(nc.sbuf_tensor("wmb", [P, 8], F32)))
    selrows_sb = [ap(En(nc.sbuf_tensor(f"selrows_sb{j}", [8, 512], BF16))) for j in range(2)]
    wsel = ap(En(nc.sbuf_tensor("wsel", [P, NT * 16], BF16)))
    scstore = ap(En(nc.sbuf_tensor("scstore", [P, NT * 8], F32)))
    selstore = ap(En(nc.sbuf_tensor("selstore", [P, NT * 8], BF16)))
    m12 = ap(En(nc.sbuf_tensor("m12", [P, NT * 2], F32)))
    den32 = ap(En(nc.sbuf_tensor("den32", [P, NT], F32)))
    rin32 = ap(En(nc.sbuf_tensor("rin32", [P, NT], F32)))
    sumx = ap(En(nc.sbuf_tensor("sumx", [P, NT], F32)))
    sumsq = ap(En(nc.sbuf_tensor("sumsq", [P, NT], F32)))
    m2c = ap(En(nc.sbuf_tensor("m2c", [P, NT], F32)))
    varc = ap(En(nc.sbuf_tensor("varc", [P, NT], F32)))
    lnvc = ap(En(nc.sbuf_tensor("lnvc", [P, NT], F32)))
    invc = ap(En(nc.sbuf_tensor("invc", [P, NT], F32)))
    nmic = ap(En(nc.sbuf_tensor("nmic", [P, NT], F32)))
    xtile = [ap(En(nc.sbuf_tensor(f"xtile{j}", [P, 1024], F32))) for j in range(3)]
    xn_tm = [ap(En(nc.sbuf_tensor(f"xntm{j}", [P, 1024], BF16))) for j in range(2)]
    obuf = [ap(En(nc.sbuf_tensor(f"obuf{j}", [P, 1024], F32))) for j in range(1)]
    ones8 = ap(En(nc.sbuf_tensor("ones8", [P, 8], F32)))
    lg2 = [ap(En(nc.sbuf_tensor(f"lg{j}", [P, 8], F32))) for j in range(2)]
    teq = [ap(En(nc.sbuf_tensor(f"teq{j}", [P, 8], F32))) for j in range(2)]
    tcum = [ap(En(nc.sbuf_tensor(f"tcum{j}", [P, 8], F32))) for j in range(2)]
    tcum2 = [ap(En(nc.sbuf_tensor(f"tcum2{j}", [P, 8], F32))) for j in range(2)]
    tf1 = [ap(En(nc.sbuf_tensor(f"tf1{j}", [P, 8], F32))) for j in range(2)]
    ts2 = [ap(En(nc.sbuf_tensor(f"ts2{j}", [P, 8], F32))) for j in range(2)]
    twn = [ap(En(nc.sbuf_tensor(f"twn{j}", [P, 8], F32))) for j in range(2)]
    dr_b = [ap(En(nc.sbuf_tensor(f"dr{j}", [64, CH], BF16))) for j in range(2)]
    bcsb = [ap(En(nc.sbuf_tensor(f"bcsb{j}", [32, CH], BF16))) for j in range(2)]
    bcast = [ap(En(nc.sbuf_tensor(f"bcast{j}", [P, CH], BF16))) for j in range(3)]
    ccast = [ap(En(nc.sbuf_tensor(f"ccast{j}", [P, CH], BF16))) for j in range(3)]
    delta_b = [ap(En(nc.sbuf_tensor(f"delta{j}", [P, CH], BF16))) for j in range(2)]
    t1_b = [ap(En(nc.sbuf_tensor(f"t1{j}", [P, CH], BF16))) for j in range(2)]
    u_b = [ap(En(nc.sbuf_tensor(f"u{j}", [P, CH], BF16))) for j in range(2)]
    da_b = [ap(En(nc.sbuf_tensor(f"da{j}", [P, CH], F32))) for j in range(2)]
    dbu_b = ap(En(nc.sbuf_tensor("dbu", [P, CH], BF16)))
    hpair = ap(En(nc.sbuf_tensor("hpair", [P, 2 * CH], F32)))
    p_b = [ap(En(nc.sbuf_tensor(f"pb{j}", [P, CH], BF16))) for j in range(2)]
    xnd_b = [ap(En(nc.sbuf_tensor(f"xnd{j}", [P, CH], BF16))) for j in range(2)]
    hstate = ap(En(nc.sbuf_tensor("hstate", [P, 128], F32)))
    g_b = [ap(En(nc.sbuf_tensor(f"gb{j}", [P, 512], BF16))) for j in range(2)]
    u2_b = [ap(En(nc.sbuf_tensor(f"u2{j}", [P, 512], BF16))) for j in range(2)]
    vtmp = ap(En(nc.sbuf_tensor("vtmp", [P, 512], BF16)))
    vtmp2 = ap(En(nc.sbuf_tensor("vtmp2", [P, 512], BF16)))
    tt_b = [ap(En(nc.sbuf_tensor(f"ttb{j}", [P, 512], BF16))) for j in range(2)]
    v_b = [ap(En(nc.sbuf_tensor(f"vb{j}", [P, 512], BF16))) for j in range(4)]
    xpw_bf = ap(En(nc.sbuf_tensor("xpwbf", [P, 768], BF16)))
    dtw_bf = ap(En(nc.sbuf_tensor("dtwbf", [64, 1024], BF16)))
    gw_bf = ap(En(nc.sbuf_tensor("gwbf", [P, 64], BF16)))
    acolx = ap(En(nc.sbuf_tensor("acolx", [P, 128], F32)))
    acole = ap(En(nc.sbuf_tensor("acole", [P, 128], F32)))
    dcol = ap(En(nc.sbuf_tensor("dcol", [P, 8], F32)))
    dtbcol = ap(En(nc.sbuf_tensor("dtbcol", [P, 8], F32)))
    lb_bc = ap(En(nc.sbuf_tensor("lbbc", [P, 8], F32)))
    identb_sb = ap(En(nc.sbuf_tensor("identbsb", [P, P], BF16)))
    identf_sb = ap(En(nc.sbuf_tensor("identfsb", [P, P], F32)))
    onesb_sb = ap(En(nc.sbuf_tensor("onesbsb", [P, 1], BF16)))
    ccsb = ap(En(nc.sbuf_tensor("ccsb", [16, 1], F32)))
    wstat2 = ap(En(nc.sbuf_tensor("wstat2", [8, 2], F32)))
    cntc = ap(En(nc.sbuf_tensor("cntc", [8, 1], F32)))
    touch = ap(En(nc.sbuf_tensor("touch", [P, 8], BF16)))
    lncnt = ap(En(nc.sbuf_tensor("lncnt", [8, 1], F32)))
    rcnt = ap(En(nc.sbuf_tensor("rcnt", [8, 1], F32)))
    wmean8 = ap(En(nc.sbuf_tensor("wmean8", [8, 1], F32)))
    epscol = ap(En(nc.sbuf_tensor("epscol", [P, 1], F32)))

    ps0 = ap(En(nc.psum_tensor("ps0", [P, 512], F32)))
    ps1 = ap(En(nc.psum_tensor("ps1", [P, 1024], BF16)))
    ps2 = ap(En(nc.psum_tensor("ps2", [P, 1024], BF16)))
    psg = ap(En(nc.psum_tensor("psg", [P, 512], F32)))
    psu = ap(En(nc.psum_tensor("psu", [P, 512], F32)))
    psd = ap(En(nc.psum_tensor("psd", [P, 512], F32)))
    psy = [ap(En(nc.psum_tensor("psy6", [P, 512], F32))),
           ap(En(nc.psum_tensor("psy7", [P, 512], F32)))]
    ps1f = ps1.bitcast(F32)
    ps2f = ps2.bitcast(F32)
    psyb = psy[1].bitcast(BF16)

    DMA_SEMS = (["mCst", "mLb", "mO", "mCC", "mWM8", "mWMB", "mSR0", "mSR1"]
                + [f"mX{j}" for j in range(3)] + [f"mStg{j}" for j in range(2)]
                + [f"mWC{m}" for m in range(3)] + [f"mWB{m}" for m in range(3)]
                + [f"mBC{j}" for j in range(3)] + [f"mMB{j}" for j in range(2)]
                + [f"mBcb{j}" for j in range(2)])
    ENG_SEMS = ["eS", "eV", "eT", "eG"]
    sems = {nm: En(nc.semaphore(nm)) for nm in DMA_SEMS + ENG_SEMS}
    cnt = {nm: 0 for nm in sems}
    step = {nm: (16 if nm in DMA_SEMS else 1) for nm in sems}
    ops = {k: [] for k in ("sync", "gpsimd", "scalar", "vector", "tensor")}

    _rc = getattr(build_program, "_racecheck", False)

    def _collect_aps(meth, args, kw):
        from concourse.ap import AP
        reads, writes = [], []
        wr_keys = ("out",)
        names = {"dma_start": (("out",), ("in_",)),
                 "matmul": (("out",), ("lhsT", "rhs")),
                 "transpose": (("out",), ("in_", "identity")),
                 "activation": (("out", "accum_out"), ("in_", "scale", "bias")),
                 "copy": (("out",), ("in_",)),
                 "tensor_copy": (("out",), ("in_",)),
                 "tensor_tensor": (("out",), ("in0", "in1")),
                 "tensor_mul": (("out", ), ("in0", "in1")),
                 "tensor_add": (("out", ), ("in0", "in1")),
                 "tensor_scalar": (("out", "accum_out"), ("in0", "scalar1", "scalar2")),
                 "tensor_scalar_add": (("out",), ("in0", "scalar1")),
                 "tensor_scalar_mul": (("out",), ("in0", "scalar1")),
                 "tensor_scalar_max": (("out",), ("in0", "scalar1")),
                 "scalar_tensor_tensor": (("out",), ("in0", "scalar", "in1")),
                 "tensor_tensor_scan": (("out",), ("data0", "data1", "initial")),
                 "reduce_max": (("out",), ("in_",)),
                 "memset": (("out",), ()),
                 "collective_compute": ((), ()),
                 }
        wk, rk = names.get(meth, ((), ()))
        allv = list(args) + [kw.get(k) for k in ("out", "in_", "in0", "in1",
                                                 "lhsT", "rhs", "identity",
                                                 "scalar1", "scalar2", "scalar",
                                                 "data0", "data1", "initial",
                                                 "accum_out", "scale", "bias")]
        def add(lst, v):
            if isinstance(v, AP):
                r = _ap_range(v)
                if r:
                    lst.append(r)
        for k in wk:
            add(writes, kw.get(k))
        for k in rk:
            add(reads, kw.get(k))
        if args:
            add(writes, args[0])
            for v in args[1:]:
                add(reads, v)
        return reads, writes

    def em(eng, meth, *args, inc=None, **kw):
        tick = None
        if inc is not None:
            cnt[inc] += step[inc]
            tick = (inc, cnt[inc])
            h, v = sems[inc], step[inc]

            def f(e, meth=meth, args=args, kw=kw, h=h, v=v):
                getattr(e, meth)(*args, **kw).then_inc(h, v)
        else:
            def f(e, meth=meth, args=args, kw=kw):
                getattr(e, meth)(*args, **kw)
        ops[eng].append(f)
        if _rc:
            r, w = _collect_aps(meth, args, kw)
            RACECHECK.append(("op", eng, meth, r, w,
                              None if tick is None else tick))
        return tick

    def wait(eng, guard):
        if guard is None:
            return
        nm, v = guard
        if v <= 0:
            return
        ops[eng].append(lambda e, s=sems[nm], v=v: e.wait_ge(s, v))
        if _rc:
            RACECHECK.append(("wait", eng, nm, v))

    def gmax(*guards):
        out = {}
        for g in guards:
            if g is None:
                continue
            nm, v = g
            out[nm] = max(out.get(nm, 0), v)
        return [(nm, v) for nm, v in out.items()]

    def waits(eng, *guards):
        for g in gmax(*guards):
            wait(eng, g)

    for (dst, src) in ((wstage[0][:, 0:768], xpw_d),
                       (wstage[1][0:64, 0:1024], dtw_d),
                       (wstage[0][:, 768:832], gw_d),
                       (acole, alog_d),
                       (dcol, dvec_d),
                       (dtbcol, dtb_d),
                       (identb_sb, identb_d),
                       (identf_sb, identf_d),
                       (onesb_sb, onesb_d)):
        t_const = em("sync", "dma_start", inc="mCst", out=dst, in_=src)
    g_lb = em("gpsimd", "dma_start", inc="mLb", out=lb_bc, in_=lb_d.partition_broadcast(P))
    wait("scalar", t_const)
    em("scalar", "copy", out=xpw_bf, in_=wstage[0][:, 0:768])
    em("scalar", "copy", out=dtw_bf, in_=wstage[1][0:64, 0:1024])
    em("scalar", "copy", out=gw_bf, in_=wstage[0][:, 768:832])
    t_sc = em("scalar", "activation", inc="eS", out=acolx, in_=acole, func=AF.Exp)
    wait("vector", t_sc)
    em("vector", "tensor_scalar_mul", acole, acolx, -1.0)
    em("vector", "memset", epscol, 1e-5)
    em("vector", "memset", ones8, 1.0, inc="eV")
    acol = acole
    consts_eS = t_sc
    consts_eV = ("eV", cnt["eV"])

    cast_st = {"stage": [consts_eS, consts_eS], "wcw": [None, None, None]}
    cast_final = [None, None, None]

    def cast_gen():
        for pid in PASS_ORDER:
            for m in range(3):
                for hf in range(4):
                    si = (4 * m + hf) % 2
                    waits("sync", cast_st["stage"][si])
                    mst = em("sync", "dma_start", inc=f"mStg{si}", out=wstage[si],
                             in_=ws_d[pid, m, :, hf * 1024:(hf + 1) * 1024])
                    waits("scalar", mst, cast_st["wcw"][m])
                    cast_st["stage"][si] = em("scalar", "copy", inc="eS",
                                              out=w_bf[m][:, hf * 1024:(hf + 1) * 1024],
                                              in_=wstage[si])
                    yield
                wl = em("scalar", "copy", inc="eS", out=touch[:, m:m + 1],
                        in_=w_bf[m][:, 4095:4096])
                wait("gpsimd", wl)
                wcw = em("gpsimd", "dma_start", inc=f"mWC{m}",
                         out=wcache_d[pid, m], in_=w_bf[m])
                cast_st["wcw"][m] = wcw
                cast_final[m] = wcw
                yield

    wc_final = [None, None, None]

    xload = [None] * NT
    s3t = [None] * NT
    v1t = [None] * NT
    s5t = [None] * NT
    v2t = [None] * NT
    s6t = [None] * NT
    t1t = [None] * NT
    evt = [None] * NT

    def p0a_load(k):
        if k >= 3:
            wait("sync", s6t[k - 3])
        xload[k] = em("sync", "dma_start", inc=f"mX{k % 3}", out=xtile[k % 3],
                      in_=x_d[k * P:(k + 1) * P, :])

    def p0a_A(k):
        wait("scalar", xload[k])
        em("scalar", "activation", out=obuf[0], in_=xtile[k % 3], func=AF.Square,
           accum_out=sumsq[:, k:k + 1])
        em("scalar", "activation", out=obuf[0], in_=xtile[k % 3], func=AF.Copy,
           scale=float(1.0 / 1024.0), accum_out=sumx[:, k:k + 1])
        s3t[k] = em("scalar", "activation", inc="eS", out=m2c[:, k:k + 1],
                    in_=sumx[:, k:k + 1], func=AF.Square)
        wait("vector", s3t[k])
        v1t[k] = em("vector", "tensor_scalar", inc="eV", out=varc[:, k:k + 1],
                    in0=sumsq[:, k:k + 1], scalar1=float(1.0 / 1024.0),
                    scalar2=m2c[:, k:k + 1], op0=ALU.mult, op1=ALU.subtract)

    def p0a_B(k):
        wait("scalar", v1t[k])
        waits("scalar", consts_eV)
        em("scalar", "activation", out=lnvc[:, k:k + 1], in_=varc[:, k:k + 1],
           func=AF.Ln, bias=epscol)
        s5t[k] = em("scalar", "activation", inc="eS", out=invc[:, k:k + 1],
                    in_=lnvc[:, k:k + 1], func=AF.Exp, scale=-0.5)

    def p0a_C(k):
        waits("vector", s5t[k], evt[k - 2] if k >= 2 else None)
        s6t[k] = em("vector", "tensor_scalar", inc="eV", out=xn_tm[k % 2],
                    in0=xtile[k % 3], scalar1=sumx[:, k:k + 1],
                    scalar2=invc[:, k:k + 1], op0=ALU.subtract, op1=ALU.mult)
        pst = ps1 if (k % 2 == 0) else ps2
        waits("tensor", s6t[k], evt[k - 2] if k >= 2 else None)
        for h in range(HCH):
            last = (h == HCH - 1)
            t1t[k] = em("tensor", "transpose", inc="eT" if last else None,
                        out=pst[:, h * 128:(h + 1) * 128],
                        in_=xn_tm[k % 2][:, h * 128:(h + 1) * 128],
                        identity=identb_sb) or t1t[k]
        wait("vector", t1t[k])
        for h in range(HCH):
            last = (h == HCH - 1)
            evt[k] = em("vector", "tensor_copy", inc="eV" if last else None,
                        out=xn_fm[h][:, k * 128:(k + 1) * 128],
                        in_=pst[:, h * 128:(h + 1) * 128]) or evt[k]

    cg = cast_gen()
    cg_alive = True
    for k in range(NT + 2):
        if k < NT:
            p0a_load(k)
            p0a_A(k)
        if 1 <= k <= NT:
            p0a_B(k - 1)
        if k >= 2:
            p0a_C(k - 2)
        if cg_alive:
            for _ in range(5):
                try:
                    next(cg)
                except StopIteration:
                    cg_alive = False
                    break
    while cg_alive:
        try:
            next(cg)
        except StopIteration:
            cg_alive = False
    for m in range(3):
        wc_final[m] = (f"mWC{m}", cnt[f"mWC{m}"])
    p0a_S_done = ("eS", cnt["eS"])
    p0a_V_done = ("eV", cnt["eV"])

    lgt = [None] * NT
    sct = [None] * NT
    tkt = [None] * NT
    selevt = [None] * (NT // 4)
    selrow_dma = [None] * (NT // 4)

    def gate_mm(k):
        psb = ps1f if (k % 2 == 0) else ps2f
        waits("tensor", evt[NT - 1] if k < 2 else lgt[k - 2], evt[k], consts_eS)
        t = None
        for h in range(HCH):
            last = (h == HCH - 1)
            t = em("tensor", "matmul", inc="eT" if last else None,
                   out=psb[:, 0:8], lhsT=xn_fm[h][:, k * 128:(k + 1) * 128],
                   rhs=gw_bf[:, h * 8:(h + 1) * 8],
                   start=(h == 0), stop=last) or t
        return t

    def topk_seq(k, j):
        sc = scstore[:, k * 8:(k + 1) * 8]
        sel = selstore[:, k * 8:(k + 1) * 8]
        return [
            ("reduce_max", dict(out=m12[:, 2 * k:2 * k + 1], in_=sc, axis=AX.X)),
            ("tensor_scalar", dict(out=teq[j], in0=sc, scalar1=m12[:, 2 * k:2 * k + 1],
                                   scalar2=None, op0=ALU.is_equal)),
            ("tensor_tensor_scan", dict(out=tcum[j], data0=ones8, data1=teq[j],
                                        initial=0.0, op0=ALU.mult, op1=ALU.add)),
            ("tensor_scalar", dict(out=tcum2[j], in0=tcum[j], scalar1=1.0,
                                   scalar2=None, op0=ALU.is_equal)),
            ("tensor_mul", dict(out=tf1[j], in0=teq[j], in1=tcum2[j])),
            ("scalar_tensor_tensor", dict(out=ts2[j], in0=tf1[j], scalar=-1e30,
                                          in1=sc, op0=ALU.mult, op1=ALU.add)),
            ("reduce_max", dict(out=m12[:, 2 * k + 1:2 * k + 2], in_=ts2[j], axis=AX.X)),
            ("tensor_scalar", dict(out=teq[j], in0=ts2[j],
                                   scalar1=m12[:, 2 * k + 1:2 * k + 2],
                                   scalar2=None, op0=ALU.is_equal)),
            ("tensor_tensor_scan", dict(out=tcum[j], data0=ones8, data1=teq[j],
                                        initial=0.0, op0=ALU.mult, op1=ALU.add)),
            ("tensor_scalar", dict(out=tcum2[j], in0=tcum[j], scalar1=1.0,
                                   scalar2=None, op0=ALU.is_equal)),
            ("tensor_mul", dict(out=twn[j], in0=teq[j], in1=tcum2[j])),
            ("tensor_add", dict(out=sel, in0=tf1[j], in1=twn[j])),
            ("tensor_copy", dict(out=wsel[:, k * 16 + 8:k * 16 + 16], in_=sel)),
        ]

    for pr in range(NT // 2):
        a, b = 2 * pr, 2 * pr + 1
        ta = gate_mm(a)
        tb = gate_mm(b)
        waits("vector", ta, g_lb, consts_eV)
        lgt[a] = em("vector", "tensor_add", inc="eV", out=lg2[0], in0=ps1f[:, 0:8], in1=lb_bc)
        wait("vector", tb)
        lgt[b] = em("vector", "tensor_add", inc="eV", out=lg2[1], in0=ps2f[:, 0:8], in1=lb_bc)
        wait("scalar", lgt[a])
        em("scalar", "activation", out=scstore[:, a * 8:(a + 1) * 8], in_=lg2[0], func=AF.Sigmoid)
        wait("scalar", lgt[b])
        sct[a] = sct[b] = em("scalar", "activation", inc="eS",
                             out=scstore[:, b * 8:(b + 1) * 8], in_=lg2[1], func=AF.Sigmoid)
        wait("vector", sct[a])
        qa, qb = topk_seq(a, 0), topk_seq(b, 1)
        for i, ((ma_, kwa), (mb_, kwb)) in enumerate(zip(qa, qb)):
            last = (i == len(qa) - 1)
            em("vector", ma_, **kwa)
            tkt[a] = tkt[b] = em("vector", mb_, inc="eV" if last else None, **kwb) or tkt[a]
        for k in (a, b):
            g4 = k // 4
            waits("tensor", tkt[k], selevt[g4 - 1] if (k % 4 == 0 and g4 >= 1) else None)
            tt = em("tensor", "transpose", inc="eT",
                    out=psyb[0:8, (k % 4) * 128:(k % 4 + 1) * 128],
                    in_=wsel[:, k * 16 + 8:k * 16 + 16], identity=identb_sb)
            if k % 4 == 3:
                waits("scalar", tt, selevt[g4 - 2] and None)
                if g4 >= 2:
                    wait("scalar", selrow_dma[g4 - 2])
                selevt[g4] = em("scalar", "copy", inc="eS",
                                out=selrows_sb[g4 % 2],
                                in_=psyb[0:8, 0:512])
                wait("gpsimd", selevt[g4])
                selrow_dma[g4] = em("gpsimd", "dma_start", inc=f"mSR{g4 % 2}",
                   out=selrows_d[:, g4 * 512:(g4 + 1) * 512], in_=selrows_sb[g4 % 2])

    vden = em("vector", "scalar_tensor_tensor", inc="eV", out=den32,
              in0=m12[:, 0:64:2], scalar=1e-8, in1=m12[:, 1:64:2],
              op0=ALU.add, op1=ALU.add)
    wait("scalar", vden)
    em("scalar", "activation", out=lnvc, in_=den32, func=AF.Ln)
    srin = em("scalar", "activation", inc="eS", out=rin32, in_=lnvc, func=AF.Exp,
              scale=-1.0)
    wait("vector", srin)
    wt = [None] * NT
    for pr in range(NT // 2):
        a, b = 2 * pr, 2 * pr + 1
        for j, k in ((0, a), (1, b)):
            em("vector", "tensor_mul", out=twn[j], in0=selstore[:, k * 8:(k + 1) * 8],
               in1=scstore[:, k * 8:(k + 1) * 8])
        for j, k in ((0, a), (1, b)):
            wt[k] = em("vector", "tensor_scalar", inc="eV" if k == b else None,
                       out=wsel[:, k * 16:k * 16 + 8], in0=twn[j],
                       scalar1=rin32[:, k:k + 1], scalar2=None, op0=ALU.mult)
        wt[a] = wt[b]
    for k in range(NT):
        if k == 0:
            wait("tensor", wt[NT - 1])
        tpart = em("tensor", "matmul", inc="eT" if k == NT - 1 else None,
                   out=psy[0][0:16, 0:1], lhsT=wsel[:, k * 16:(k + 1) * 16],
                   rhs=onesb_sb, start=(k == 0), stop=(k == NT - 1))
    wait("scalar", tpart)
    spart = em("scalar", "copy", inc="eS", out=ccsb, in_=psy[0][0:16, 0:1])
    wait("gpsimd", spart)
    em("gpsimd", "dma_start", inc="mCC", out=ccin_d.ap().rearrange("1 p -> p 1"), in_=ccsb)
    wait("gpsimd", ("mCC", cnt["mCC"]))
    gcc = em("gpsimd", "collective_compute", "AllReduce", ALU.add, inc="eG",
             replica_groups=[list(range(8))], ins=[ccin_d.ap().opt()],
             outs=[ccout_d.ap().opt()])
    wait("gpsimd", gcc)
    em("gpsimd", "dma_start", inc="mCC", out=wstat2[:, 0:1],
       in_=ccout_d.ap()[0:1, 0:8].rearrange("1 p -> p 1"))
    mws = em("gpsimd", "dma_start", inc="mCC", out=wstat2[:, 1:2],
             in_=ccout_d.ap()[0:1, 8:16].rearrange("1 p -> p 1"))
    wait("vector", mws)
    vcnt = em("vector", "tensor_scalar_max", cntc, wstat2[:, 1:2], 1.0, inc="eV")
    wait("scalar", vcnt)
    em("scalar", "activation", out=lncnt, in_=cntc, func=AF.Ln)
    srcnt = em("scalar", "activation", inc="eS", out=rcnt, in_=lncnt, func=AF.Exp,
               scale=-1.0)
    wait("vector", srcnt)
    vwm = em("vector", "scalar_tensor_tensor", inc="eV", out=wmean8,
             in0=wstat2[:, 0:1], scalar=0.5, in1=rcnt, op0=ALU.mult, op1=ALU.mult)
    wait("gpsimd", vwm)
    em("gpsimd", "dma_start", inc="mWM8", out=wm8_d.rearrange("1 p -> p 1"), in_=wmean8)
    wait("gpsimd", ("mWM8", cnt["mWM8"]))
    for e in range(8):
        em("gpsimd", "dma_start", inc="mWMB", out=wmb[:, e:e + 1],
           in_=wm8_d[0:1, e:e + 1].partition_broadcast(P))
    g_wmb = ("mWMB", cnt["mWMB"])
    g_selrow = [("mSR0", cnt["mSR0"]), ("mSR1", cnt["mSR1"])]

    p_b.append(wsel[:, 0:512])
    p_b.append(scstore.bitcast(BF16)[:, 0:512])
    da_b.append(wstage[0][:, 0:512])
    da_b.append(wstage[0][:, 512:1024])
    st = {
        "dr": [None, None], "bcsbw": [None, None],
        "bcast": [None] * 3, "ccast": [None] * 3,
        "delta": [None, None], "u": [None, None], "da": [None] * 2,
        "p": [None] * 2, "xndT": [None, None],
        "bank0": spart, "bank6": spart, "bank7": selevt[-1],
        "pst": [tpart, tpart],
        "bank3": None, "bank3b": lgt[NT - 1], "bank3v": None, "bank4": None,
        "bank4b": lgt[NT - 1], "bank5": None,
        "g": [None, None], "v": [None] * 4, "mb": [None, None],
        "wbf_used": [("mWC0", cnt["mWC0"]), ("mWC1", cnt["mWC1"]),
                     ("mWC2", cnt["mWC2"])],
        "xt": [s6t[30], s6t[31], s6t[29]],
        "ob": [p0a_S_done],
        "b12": None, "epi_t": None,
    }

    def mamba_gen(q):
        for ci in range(2):
            yield from mamba_chunk(q, ci)

    def mamba_chunk(q, ci):
        c = 2 * q + ci
        tok0 = c * CH
        ji = c % 2
        waits("tensor", st["bank0"], p0a_V_done)
        txp = None
        for h in range(HCH):
            last = (h == HCH - 1)
            txp = em("tensor", "matmul", inc="eT" if last else None,
                     out=ps0[0:96, :], lhsT=xpw_bf[:, h * 96:(h + 1) * 96],
                     rhs=xn_fm[h][:, tok0:tok0 + CH],
                     start=(h == 0), stop=last) or txp
        waits("scalar", txp, st["dr"][ji], st["bcsbw"][ji])
        em("scalar", "copy", out=dr_b[ji], in_=ps0[0:64, :])
        sdr = em("scalar", "copy", inc="eS", out=bcsb[ji], in_=ps0[64:96, :])
        st["bank0"] = sdr
        wait("gpsimd", sdr)
        gw_bc = em("gpsimd", "dma_start", inc=f"mBcb{ji}", out=bcb_d[c % 8], in_=bcsb[ji])
        st["bcsbw"][ji] = gw_bc
        yield
        for pair in range(4):
            dAc, dBc = 2 * pair, 2 * pair + 1
            for di, dch in ((0, dAc), (1, dBc)):
                waits("tensor", st["bank0"], sdr)
                td = em("tensor", "matmul", inc="eT", out=ps0[:, :],
                        lhsT=dtw_bf[:, dch * 128:(dch + 1) * 128], rhs=dr_b[ji],
                        start=True, stop=True)
                st["dr"][ji] = td
                waits("scalar", td, st["u"][di])
                em("scalar", "activation", out=t1_b[di], in_=ps0[:, :], func=AF.Exp,
                   bias=dtbcol[:, dch:dch + 1])
                sdelta = em("scalar", "activation", inc="eS", out=delta_b[di],
                            in_=t1_b[di], func=AF.Ln, bias=1.0)
                st["bank0"] = sdelta
                wait("vector", sdelta)
                st["u"][di] = em("vector", "tensor_mul", inc="eV", out=u_b[di],
                                 in0=delta_b[di], in1=xn_fm[dch][:, tok0:tok0 + CH])
                waits("scalar", st["xndT"][di])
                sxnd = em("scalar", "activation", inc="eS", out=xnd_b[di],
                          in_=xn_fm[dch][:, tok0:tok0 + CH], func=AF.Copy,
                          scale=dcol[:, dch:dch + 1])
                st["xnd"] = sxnd
                waits("tensor", sxnd, st["bank6"] if di == 0 else st["bank7"])
                st["xndT"][di] = em("tensor", "matmul", inc="eT", out=psy[di],
                                    lhsT=identb_sb, rhs=xnd_b[di],
                                    start=True, stop=False)
            yield
            svn = [None] * (N16 + 1)
            stn = [None] * (N16 + 1)
            ssn = [None] * (N16 + 1)

            def emit_da(n):
                waits("scalar", svn[n - 2] if n >= 2 else st["da"][0])
                em("scalar", "activation", out=da_b[(n % 2) * 2],
                   in_=delta_b[0], func=AF.Exp,
                   scale=acol[:, dAc * 16 + n:dAc * 16 + n + 1])
                ssn[n] = em("scalar", "activation", inc="eS",
                            out=da_b[(n % 2) * 2 + 1], in_=delta_b[1], func=AF.Exp,
                            scale=acol[:, dBc * 16 + n:dBc * 16 + n + 1])

            emit_da(0)
            for n in range(N16):
                bi = n % 3
                waits("gpsimd", gw_bc, st["bcast"][bi])
                em("gpsimd", "dma_start", inc=f"mBC{bi}", out=bcast[bi],
                   in_=bcb_d[c % 8][n:n + 1, :].partition_broadcast(P))
                mbc = em("gpsimd", "dma_start", inc=f"mBC{bi}", out=ccast[bi],
                         in_=bcb_d[c % 8][16 + n:17 + n, :].partition_broadcast(P))
                if n + 1 < N16:
                    emit_da(n + 1)
                waits("vector", ssn[n],
                      stn[n - 2] if n >= 2 else st["pst"][n])
                waits("vector", mbc)
                for di in (0, 1):
                    em("vector", "tensor_mul", out=dbu_b, in0=u_b[di], in1=bcast[bi])
                    em("vector", "tensor_tensor_scan",
                       out=hpair[:, di * CH:(di + 1) * CH],
                       data0=da_b[(n % 2) * 2 + di], data1=dbu_b,
                       initial=hstate[:, (dAc + di) * 16 + n:(dAc + di) * 16 + n + 1],
                       op0=ALU.mult, op1=ALU.add)
                    em("vector", "tensor_mul", out=p_b[(2 * n + di) % 4],
                       in0=hpair[:, di * CH:(di + 1) * CH], in1=ccast[bi])
                svn[n] = em("vector", "tensor_copy", inc="eV",
                            out=hstate[:, 32 * pair + n:32 * pair + n + 17:16],
                            in_=hpair[:, CH - 1:2 * CH:CH])
                st["bcast"][bi] = svn[n]
                if n == N16 - 1:
                    st["da"][0] = svn[N16 - 2]
                wait("tensor", svn[n])
                em("tensor", "matmul", out=psy[0], lhsT=identb_sb,
                   rhs=p_b[(2 * n) % 4], start=False, stop=(n == N16 - 1))
                stn[n] = em("tensor", "matmul", inc="eT", out=psy[1],
                            lhsT=identb_sb, rhs=p_b[(2 * n + 1) % 4],
                            start=False, stop=(n == N16 - 1))
                yield
            st["pst"] = [stn[N16 - 2], stn[N16 - 1]]
            for di, dch in ((0, dAc), (1, dBc)):
                wait("vector", stn[N16 - 1])
                vacc = em("vector", "tensor_tensor", inc="eV",
                          out=acc[dch][:, ci * CH:(ci + 1) * CH],
                          in0=acc[dch][:, ci * CH:(ci + 1) * CH], in1=psy[di],
                          op=ALU.add)
                if di == 0:
                    st["bank6"] = vacc
                else:
                    st["bank7"] = vacc

    def ffn_gen(q):
        for pi, pid in enumerate(PASS_ORDER):
            yield from ffn_pass(q, pi, pid)

    def ffn_pass(q, pi, pid):
        masked = pid < 8
        wload = [None, None, None]
        for m in range(3):
            waits("sync", st["wbf_used"][m], wc_final[m])
            wload[m] = em("sync", "dma_start", inc=f"mWB{m}", out=w_bf[m],
                          in_=wcache_d[pid, m])
        yield
        if masked:
            mi = pi % 2
            waits("gpsimd", g_selrow[0], g_selrow[1], st["mb"][mi])
            mmb = em("gpsimd", "dma_start", inc=f"mMB{mi}", out=mbt[mi],
                     in_=selrows_d[pid:pid + 1, q * QT:(q + 1) * QT].partition_broadcast(P))
            waits("scalar", g_wmb)
        for ti in range(2):
            tok0 = q * QT + ti * 512
            for ich in range(4):
                gi = ich % 2
                gpsum = psg if gi == 0 else ps1f
                upsum = psu if gi == 0 else ps2f
                bk3 = "bank3" if gi == 0 else "bank3b"
                bk4 = "bank4" if gi == 0 else "bank4b"
                waits("tensor", wload[0], wload[1], st[bk3],
                      st["bank3v"] if gi == 0 else st["b12"], p0a_V_done)
                tg = None
                for h in range(HCH):
                    last = (h == HCH - 1)
                    tg = em("tensor", "matmul", inc="eT" if last else None, out=gpsum,
                            lhsT=w_bf[0][:, h * 512 + ich * 128:h * 512 + (ich + 1) * 128],
                            rhs=xn_fm[h][:, tok0:tok0 + 512],
                            start=(h == 0), stop=last) or tg
                waits("tensor", st[bk4])
                tu = None
                for h in range(HCH):
                    last = (h == HCH - 1)
                    tu = em("tensor", "matmul", inc="eT" if last else None, out=upsum,
                            lhsT=w_bf[1][:, h * 512 + ich * 128:h * 512 + (ich + 1) * 128],
                            rhs=xn_fm[h][:, tok0:tok0 + 512],
                            start=(h == 0), stop=last) or tu
                waits("scalar", tg, st["g"][gi])
                em("scalar", "copy", out=g_b[gi], in_=gpsum)
                st[bk3] = em("scalar", "activation", inc="eS", out=tt_b[gi],
                             in_=gpsum, func=AF.Tanh, scale=0.5)
                waits("scalar", tu)
                if masked:
                    st[bk4] = em("scalar", "activation", inc="eS", out=u2_b[gi],
                                 in_=upsum, func=AF.Copy, scale=wmb[:, pid:pid + 1])
                else:
                    st[bk4] = em("scalar", "activation", inc="eS", out=u2_b[gi],
                                 in_=upsum, func=AF.Copy, scale=0.5)
                waits("vector", st[bk4], st["v"][ich],
                      mmb if (masked and ti == 0 and ich == 0) else None)
                em("vector", "tensor_mul", out=vtmp, in0=g_b[gi], in1=u2_b[gi])
                if masked:
                    em("vector", "scalar_tensor_tensor", out=vtmp2, in0=tt_b[gi],
                       scalar=1.0, in1=mbt[pi % 2][:, ti * 512:(ti + 1) * 512],
                       op0=ALU.add, op1=ALU.mult)
                    vv = em("vector", "tensor_mul", inc="eV", out=v_b[ich],
                            in0=vtmp, in1=vtmp2)
                else:
                    em("vector", "tensor_scalar_add", vtmp2, tt_b[gi], 1.0)
                    vv = em("vector", "tensor_mul", inc="eV", out=v_b[ich],
                            in0=vtmp, in1=vtmp2)
                st["g"][gi] = vv
                if masked and ti == 1 and ich == 3:
                    st["mb"][pi % 2] = vv
                if ich == 3 and ti == 1:
                    st["wbf_used"][0] = ("eT", cnt["eT"])
                    st["wbf_used"][1] = ("eT", cnt["eT"])
                if ich == 3:
                    vlast_t = vv
                yield
            vlast = vlast_t
            for h in range(HCH):
                bank = psd if (h % 2 == 0) else psg
                bkey = "bank5" if (h % 2 == 0) else "bank3v"
                waits("tensor", vlast, st[bkey], wload[2],
                      st["bank3"] if h % 2 == 1 else None)
                tdn = None
                for ich in range(4):
                    last = (ich == 3)
                    tdn = em("tensor", "matmul", inc="eT" if last else None, out=bank,
                             lhsT=w_bf[2][:, ich * 1024 + h * 128:ich * 1024 + (h + 1) * 128],
                             rhs=v_b[ich], start=(ich == 0), stop=last) or tdn
                if h == HCH - 1:
                    for ich in range(4):
                        st["v"][ich] = tdn
                    if ti == 1:
                        st["wbf_used"][2] = tdn
                wait("vector", tdn)
                st[bkey] = em("vector", "tensor_tensor", inc="eV",
                              out=acc[h][:, ti * 512:(ti + 1) * 512],
                              in0=acc[h][:, ti * 512:(ti + 1) * 512], in1=bank,
                              op=ALU.add)
                if h % 2 == 1:
                    yield

    def epilogue(q, acc_done):
        for tb in range(8):
            gt = q * 8 + tb
            xi = tb % 3
            waits("sync", st["xt"][xi])
            mxr = em("sync", "dma_start", inc=f"mX{xi}", out=xtile[xi],
                     in_=x_d[gt * P:(gt + 1) * P, :])
            waits("tensor", acc_done, st["b12"], st["bank3b"], st["bank4b"])
            ttr = None
            for h in range(HCH):
                psb = ps1f if h < 4 else ps2f
                last = (h == HCH - 1)
                ttr = em("tensor", "transpose", inc="eT" if last else None,
                         out=psb[:, (h % 4) * 128:(h % 4 + 1) * 128],
                         in_=acc[h][:, tb * 128:(tb + 1) * 128],
                         identity=identf_sb) or ttr
            oi = 0
            waits("vector", ttr, mxr, st["ob"][oi])
            em("vector", "tensor_tensor", out=obuf[oi][:, 0:512], in0=ps1f,
               in1=xtile[xi][:, 0:512], op=ALU.add)
            vep = em("vector", "tensor_tensor", inc="eV", out=obuf[oi][:, 512:1024],
                     in0=ps2f, in1=xtile[xi][:, 512:1024], op=ALU.add)
            st["b12"] = vep
            st["xt"][xi] = vep
            wait("sync", vep)
            em("sync", "dma_start", inc="mO", out=out_d[gt * P:(gt + 1) * P, :],
               in_=obuf[oi])
            st["ob"][oi] = ("mO", cnt["mO"])

    for q in range(NQ):
        for h in range(HCH):
            waits("vector", st["epi_t"])
            em("vector", "memset", acc[h], 0.0)
        if q % 2 == 0:
            em("vector", "memset", hstate, 0.0)
        em("vector", "nop", inc="eV")
        INTERLEAVE = globals().get("_INTERLEAVE", True)
        gm = mamba_gen(q)
        gf = ffn_gen(q)
        alive_m, alive_f = True, True
        while alive_m or alive_f:
            if alive_f:
                for _ in range(6 if INTERLEAVE else 10 ** 9):
                    try:
                        next(gf)
                    except StopIteration:
                        alive_f = False
                        break
            if alive_m:
                for _ in range(4 if INTERLEAVE else 10 ** 9):
                    try:
                        next(gm)
                    except StopIteration:
                        alive_m = False
                        break
        acc_done = ("eV", cnt["eV"])
        epilogue(q, acc_done)
        st["epi_t"] = ("eT", cnt["eT"])

    wait("gpsimd", ("mO", cnt["mO"]))

    block = En(nc.Block())

    @block.sync
    def _(e):
        for f in ops["sync"]:
            f(e)

    @block.gpsimd
    def _(e):
        for f in ops["gpsimd"]:
            f(e)

    @block.scalar
    def _(e):
        for f in ops["scalar"]:
            f(e)

    @block.vector
    def _(e):
        for f in ops["vector"]:
            f(e)

    @block.tensor
    def _(e):
        for f in ops["tensor"]:
            f(e)

    stack.pop_all()
    return nc


def _host_prep(inputs):
    f32 = np.float32
    gw8 = np.asarray(inputs["L0_ffn_expert_gate_w"], f32)
    uw8 = np.asarray(inputs["L0_ffn_expert_up_w"], f32)
    dw8 = np.asarray(inputs["L0_ffn_expert_down_w"], f32)
    sg = np.asarray(inputs["L0_ffn_shared_gate_w"], f32)
    su = np.asarray(inputs["L0_ffn_shared_up_w"], f32)
    sd = np.asarray(inputs["L0_ffn_shared_down_w"], f32)

    def packT(w):
        wt = np.ascontiguousarray(w.T)
        I, O = wt.shape
        return np.ascontiguousarray(
            wt.reshape(I // 128, 128, O).transpose(1, 0, 2).reshape(128, (I // 128) * O))

    wstream = np.empty((NPASS, 3, P, 4096), f32)
    for e in range(8):
        wstream[e, 0] = packT(gw8[e])
        wstream[e, 1] = packT(uw8[e])
        wstream[e, 2] = packT(dw8[e])
    wstream[8, 0] = packT(sg[0:512])
    wstream[8, 1] = packT(su[0:512])
    wstream[8, 2] = packT(sd[:, 0:512])
    wstream[9, 0] = packT(sg[512:1024])
    wstream[9, 1] = packT(su[512:1024])
    wstream[9, 2] = packT(sd[:, 512:1024])

    xpw = packT(np.asarray(inputs["L0_attn_x_proj_w"], f32))
    dtw = np.ascontiguousarray(np.asarray(inputs["L0_attn_dt_proj_w"], f32).T)
    alog = np.ascontiguousarray(
        np.asarray(inputs["L0_attn_A_log"], f32).reshape(8, 128, 16)
        .transpose(1, 0, 2).reshape(128, 128))
    dvec = np.ascontiguousarray(
        np.asarray(inputs["L0_attn_D"], f32).reshape(8, 128).T)
    dtb = np.ascontiguousarray(
        np.asarray(inputs["L0_attn_dt_proj_b"], f32).reshape(8, 128).T)
    gwm = packT(np.asarray(inputs["L0_ffn_gate_w"], f32))
    lb = np.asarray(inputs["L0_ffn_lb_bias"], f32).reshape(1, 8)
    identb = np.eye(P, dtype=BF)
    identf = np.eye(P, dtype=np.float32)
    onesb = np.ones((P, 1), dtype=BF)
    return dict(wstream=wstream, xpw=xpw, dtw=dtw, alog=alog, dvec=dvec, dtb=dtb,
                gw=gwm, lb=lb, identb=identb, identf=identf, onesb=onesb)


def kernel(**inputs):
    x = np.asarray(inputs["L0_attn_x"], np.float32)
    shared = _host_prep(inputs)
    nc = build_program()
    in_maps = []
    for c in range(8):
        m = dict(shared)
        m["x"] = np.ascontiguousarray(x[2 * c:2 * c + 2].reshape(TCORE, 1024))
        in_maps.append(m)
    res = run_bass_kernel_spmd(nc, in_maps, core_ids=list(range(8)))
    out = np.concatenate([r["out"].reshape(2, 2048, 1024) for r in res.results], axis=0)
    return out.astype(np.float32)


# version 20
# speedup vs baseline: 1.1597x; 1.0543x over previous
import numpy as np
import ml_dtypes
import concourse.bass as bass
import concourse.mybir as mybir
from concourse.bass_utils import run_bass_kernel_spmd

F32 = mybir.dt.float32
BF16 = mybir.dt.bfloat16
AF = mybir.ActivationFunctionType
ALU = mybir.AluOpType
AX = mybir.AxisListType

P = 128
HCH = 8
TCORE = 4096
NT = 32
NQ = 4
QT = 1024
CH = 512
N16 = 16
NPASS = 10
PASS_ORDER = [8, 9, 0, 1, 2, 3, 4, 5, 6, 7]
BF = ml_dtypes.bfloat16
RACECHECK = []


def _ap_range(a):
    try:
        t = a.tensor
        name = t.name
        import concourse.mybir as _mb
        esz = _mb.dt.size(a.dtype)
        off = a.offset if isinstance(a.offset, int) else 0
        dims = [(int(s), int(c)) for s, c in list(a.ap)]
        if not dims:
            return (name, str(a.space), 0, 128, off * esz, off * esz + esz)
        rowstep, pcnt = dims[0]
        if rowstep <= 0:
            rowstep = 1
        plo = off // rowstep
        flo = off % rowstep
        ext = 0
        for stp, cnt in dims[1:]:
            if cnt > 1:
                ext += abs(stp) * (cnt - 1)
        return (name, str(a.space), plo, plo + pcnt, flo * esz,
                (flo + ext + 1) * esz)
    except Exception:
        return None


def build_program():
    nc = bass.Bass("TRN2", target_bir_lowering=False, debug=False)

    x_d = nc.dram_tensor("x", [TCORE, 1024], F32, kind="ExternalInput").ap()
    ws_d = nc.dram_tensor("wstream", [NPASS, 3, P, 4096], F32, kind="ExternalInput").ap()
    xpw_d = nc.dram_tensor("xpw", [P, 768], F32, kind="ExternalInput").ap()
    dtw_d = nc.dram_tensor("dtw", [64, 1024], F32, kind="ExternalInput").ap()
    alog_d = nc.dram_tensor("alog", [P, 128], F32, kind="ExternalInput").ap()
    dvec_d = nc.dram_tensor("dvec", [P, 8], F32, kind="ExternalInput").ap()
    dtb_d = nc.dram_tensor("dtb", [P, 8], F32, kind="ExternalInput").ap()
    gw_d = nc.dram_tensor("gw", [P, 64], F32, kind="ExternalInput").ap()
    lb_d = nc.dram_tensor("lb", [1, 8], F32, kind="ExternalInput").ap()
    identb_d = nc.dram_tensor("identb", [P, P], BF16, kind="ExternalInput").ap()
    identf_d = nc.dram_tensor("identf", [P, P], F32, kind="ExternalInput").ap()
    onesb_d = nc.dram_tensor("onesb", [P, 1], BF16, kind="ExternalInput").ap()
    out_d = nc.dram_tensor("out", [TCORE, 1024], F32, kind="ExternalOutput").ap()
    wcache_d = nc.dram_tensor("wcache", [NPASS, 3, P, 4096], BF16).ap()
    bcb_d = nc.dram_tensor("bcb", [8, 32, CH], BF16).ap()
    selrows_d = nc.dram_tensor("selrows", [8, TCORE], BF16).ap()
    wm8_d = nc.dram_tensor("wm8", [1, 8], F32).ap()
    ccin_d = nc.dram_tensor("ccin", [1, 16], F32)
    ccout_d = nc.dram_tensor("ccout", [1, 16], F32)

    from contextlib import ExitStack
    stack = ExitStack()
    En = stack.enter_context
    ap = lambda h: h.ap()

    xn_fm = [ap(En(nc.sbuf_tensor(f"xnfm{h}", [P, TCORE], BF16))) for h in range(HCH)]
    acc = [ap(En(nc.sbuf_tensor(f"acc{h}", [P, QT], F32))) for h in range(HCH)]
    wstage = [ap(En(nc.sbuf_tensor(f"wstage{j}", [P, 1024], F32))) for j in range(2)]
    w_bf = [ap(En(nc.sbuf_tensor(f"wbf{m}", [P, 4096], BF16))) for m in range(3)]
    mbt = [ap(En(nc.sbuf_tensor(f"mb{j}", [P, QT], BF16))) for j in range(2)]
    wmb = ap(En(nc.sbuf_tensor("wmb", [P, 8], F32)))
    selrows_sb = [ap(En(nc.sbuf_tensor(f"selrows_sb{j}", [8, 512], BF16))) for j in range(2)]
    wsel = ap(En(nc.sbuf_tensor("wsel", [P, NT * 16], BF16)))
    scstore = ap(En(nc.sbuf_tensor("scstore", [P, NT * 8], F32)))
    selstore = ap(En(nc.sbuf_tensor("selstore", [P, NT * 8], BF16)))
    m12 = ap(En(nc.sbuf_tensor("m12", [P, NT * 2], F32)))
    den32 = ap(En(nc.sbuf_tensor("den32", [P, NT], F32)))
    rin32 = ap(En(nc.sbuf_tensor("rin32", [P, NT], F32)))
    sumx = ap(En(nc.sbuf_tensor("sumx", [P, NT], F32)))
    sumsq = ap(En(nc.sbuf_tensor("sumsq", [P, NT], F32)))
    m2c = ap(En(nc.sbuf_tensor("m2c", [P, NT], F32)))
    varc = ap(En(nc.sbuf_tensor("varc", [P, NT], F32)))
    lnvc = ap(En(nc.sbuf_tensor("lnvc", [P, NT], F32)))
    invc = ap(En(nc.sbuf_tensor("invc", [P, NT], F32)))
    nmic = ap(En(nc.sbuf_tensor("nmic", [P, NT], F32)))
    xtile = [ap(En(nc.sbuf_tensor(f"xtile{j}", [P, 1024], F32))) for j in range(3)]
    xn_tm = [ap(En(nc.sbuf_tensor(f"xntm{j}", [P, 1024], BF16))) for j in range(2)]
    obuf = [ap(En(nc.sbuf_tensor(f"obuf{j}", [P, 1024], F32))) for j in range(1)]
    ones8 = ap(En(nc.sbuf_tensor("ones8", [P, 8], F32)))
    lg2 = [ap(En(nc.sbuf_tensor(f"lg{j}", [P, 8], F32))) for j in range(2)]
    teq = [ap(En(nc.sbuf_tensor(f"teq{j}", [P, 8], F32))) for j in range(2)]
    tcum = [ap(En(nc.sbuf_tensor(f"tcum{j}", [P, 8], F32))) for j in range(2)]
    tcum2 = [ap(En(nc.sbuf_tensor(f"tcum2{j}", [P, 8], F32))) for j in range(2)]
    tf1 = [ap(En(nc.sbuf_tensor(f"tf1{j}", [P, 8], F32))) for j in range(2)]
    ts2 = [ap(En(nc.sbuf_tensor(f"ts2{j}", [P, 8], F32))) for j in range(2)]
    twn = [ap(En(nc.sbuf_tensor(f"twn{j}", [P, 8], F32))) for j in range(2)]
    dr_b = [ap(En(nc.sbuf_tensor(f"dr{j}", [64, CH], BF16))) for j in range(2)]
    bcsb = [ap(En(nc.sbuf_tensor(f"bcsb{j}", [32, CH], BF16))) for j in range(2)]
    bcast = [ap(En(nc.sbuf_tensor(f"bcast{j}", [P, CH], BF16))) for j in range(3)]
    ccast = [ap(En(nc.sbuf_tensor(f"ccast{j}", [P, CH], BF16))) for j in range(3)]
    delta_b = [ap(En(nc.sbuf_tensor(f"delta{j}", [P, CH], BF16))) for j in range(2)]
    t1_b = [ap(En(nc.sbuf_tensor(f"t1{j}", [P, CH], BF16))) for j in range(2)]
    u_b = [ap(En(nc.sbuf_tensor(f"u{j}", [P, CH], BF16))) for j in range(2)]
    da_b = [ap(En(nc.sbuf_tensor(f"da{j}", [P, CH], F32))) for j in range(2)]
    dbu_b = ap(En(nc.sbuf_tensor("dbu", [P, CH], BF16)))
    hpair = ap(En(nc.sbuf_tensor("hpair", [P, 2 * CH], F32)))
    p_b = [ap(En(nc.sbuf_tensor(f"pb{j}", [P, CH], BF16))) for j in range(2)]
    xnd_b = [ap(En(nc.sbuf_tensor(f"xnd{j}", [P, CH], BF16))) for j in range(2)]
    hstate = ap(En(nc.sbuf_tensor("hstate", [P, 128], F32)))
    g_b = [ap(En(nc.sbuf_tensor(f"gb{j}", [P, 512], BF16))) for j in range(2)]
    u2_b = [ap(En(nc.sbuf_tensor(f"u2{j}", [P, 512], BF16))) for j in range(2)]
    vtmp = ap(En(nc.sbuf_tensor("vtmp", [P, 512], BF16)))
    vtmp2 = ap(En(nc.sbuf_tensor("vtmp2", [P, 512], BF16)))
    tt_b = [ap(En(nc.sbuf_tensor(f"ttb{j}", [P, 512], BF16))) for j in range(2)]
    v_b = [ap(En(nc.sbuf_tensor(f"vb{j}", [P, 512], BF16))) for j in range(4)]
    xpw_bf = ap(En(nc.sbuf_tensor("xpwbf", [P, 768], BF16)))
    dtw_bf = ap(En(nc.sbuf_tensor("dtwbf", [64, 1024], BF16)))
    gw_bf = ap(En(nc.sbuf_tensor("gwbf", [P, 64], BF16)))
    acolx = ap(En(nc.sbuf_tensor("acolx", [P, 128], F32)))
    acole = ap(En(nc.sbuf_tensor("acole", [P, 128], F32)))
    dcol = ap(En(nc.sbuf_tensor("dcol", [P, 8], F32)))
    dtbcol = ap(En(nc.sbuf_tensor("dtbcol", [P, 8], F32)))
    lb_bc = ap(En(nc.sbuf_tensor("lbbc", [P, 8], F32)))
    identb_sb = ap(En(nc.sbuf_tensor("identbsb", [P, P], BF16)))
    identf_sb = ap(En(nc.sbuf_tensor("identfsb", [P, P], F32)))
    onesb_sb = ap(En(nc.sbuf_tensor("onesbsb", [P, 1], BF16)))
    ccsb = ap(En(nc.sbuf_tensor("ccsb", [16, 1], F32)))
    wstat2 = ap(En(nc.sbuf_tensor("wstat2", [8, 2], F32)))
    cntc = ap(En(nc.sbuf_tensor("cntc", [8, 1], F32)))
    touch = ap(En(nc.sbuf_tensor("touch", [P, 8], BF16)))
    lncnt = ap(En(nc.sbuf_tensor("lncnt", [8, 1], F32)))
    rcnt = ap(En(nc.sbuf_tensor("rcnt", [8, 1], F32)))
    wmean8 = ap(En(nc.sbuf_tensor("wmean8", [8, 1], F32)))
    epscol = ap(En(nc.sbuf_tensor("epscol", [P, 1], F32)))

    ps0 = ap(En(nc.psum_tensor("ps0", [P, 512], F32)))
    ps1 = ap(En(nc.psum_tensor("ps1", [P, 1024], BF16)))
    ps2 = ap(En(nc.psum_tensor("ps2", [P, 1024], BF16)))
    psg = ap(En(nc.psum_tensor("psg", [P, 512], F32)))
    psu = ap(En(nc.psum_tensor("psu", [P, 512], F32)))
    psd = ap(En(nc.psum_tensor("psd", [P, 512], F32)))
    psy = [ap(En(nc.psum_tensor("psy6", [P, 512], F32))),
           ap(En(nc.psum_tensor("psy7", [P, 512], F32)))]
    ps1f = ps1.bitcast(F32)
    ps2f = ps2.bitcast(F32)
    psyb = psy[1].bitcast(BF16)

    DMA_SEMS = (["mCst", "mLb", "mO", "mCC", "mWM8", "mWMB", "mSR0", "mSR1"]
                + [f"mX{j}" for j in range(3)] + [f"mStg{j}" for j in range(2)]
                + [f"mWC{m}" for m in range(3)] + [f"mWB{m}" for m in range(3)]
                + [f"mBC{j}" for j in range(3)] + [f"mMB{j}" for j in range(2)]
                + [f"mBcb{j}" for j in range(2)])
    ENG_SEMS = ["eS", "eV", "eT", "eG"]
    sems = {nm: En(nc.semaphore(nm)) for nm in DMA_SEMS + ENG_SEMS}
    cnt = {nm: 0 for nm in sems}
    step = {nm: (16 if nm in DMA_SEMS else 1) for nm in sems}
    ops = {k: [] for k in ("sync", "gpsimd", "scalar", "vector", "tensor")}

    _rc = getattr(build_program, "_racecheck", False)

    def _collect_aps(meth, args, kw):
        from concourse.ap import AP
        reads, writes = [], []
        wr_keys = ("out",)
        names = {"dma_start": (("out",), ("in_",)),
                 "matmul": (("out",), ("lhsT", "rhs")),
                 "transpose": (("out",), ("in_", "identity")),
                 "activation": (("out", "accum_out"), ("in_", "scale", "bias")),
                 "copy": (("out",), ("in_",)),
                 "tensor_copy": (("out",), ("in_",)),
                 "tensor_tensor": (("out",), ("in0", "in1")),
                 "tensor_mul": (("out", ), ("in0", "in1")),
                 "tensor_add": (("out", ), ("in0", "in1")),
                 "tensor_scalar": (("out", "accum_out"), ("in0", "scalar1", "scalar2")),
                 "tensor_scalar_add": (("out",), ("in0", "scalar1")),
                 "tensor_scalar_mul": (("out",), ("in0", "scalar1")),
                 "tensor_scalar_max": (("out",), ("in0", "scalar1")),
                 "scalar_tensor_tensor": (("out",), ("in0", "scalar", "in1")),
                 "tensor_tensor_scan": (("out",), ("data0", "data1", "initial")),
                 "reduce_max": (("out",), ("in_",)),
                 "memset": (("out",), ()),
                 "collective_compute": ((), ()),
                 }
        wk, rk = names.get(meth, ((), ()))
        allv = list(args) + [kw.get(k) for k in ("out", "in_", "in0", "in1",
                                                 "lhsT", "rhs", "identity",
                                                 "scalar1", "scalar2", "scalar",
                                                 "data0", "data1", "initial",
                                                 "accum_out", "scale", "bias")]
        def add(lst, v):
            if isinstance(v, AP):
                r = _ap_range(v)
                if r:
                    lst.append(r)
        for k in wk:
            add(writes, kw.get(k))
        for k in rk:
            add(reads, kw.get(k))
        if args:
            add(writes, args[0])
            for v in args[1:]:
                add(reads, v)
        return reads, writes

    def em(eng, meth, *args, inc=None, **kw):
        tick = None
        if inc is not None:
            cnt[inc] += step[inc]
            tick = (inc, cnt[inc])
            h, v = sems[inc], step[inc]

            def f(e, meth=meth, args=args, kw=kw, h=h, v=v):
                getattr(e, meth)(*args, **kw).then_inc(h, v)
        else:
            def f(e, meth=meth, args=args, kw=kw):
                getattr(e, meth)(*args, **kw)
        ops[eng].append(f)
        if _rc:
            r, w = _collect_aps(meth, args, kw)
            RACECHECK.append(("op", eng, meth, r, w,
                              None if tick is None else tick))
        return tick

    def wait(eng, guard):
        if guard is None:
            return
        nm, v = guard
        if v <= 0:
            return
        ops[eng].append(lambda e, s=sems[nm], v=v: e.wait_ge(s, v))
        if _rc:
            RACECHECK.append(("wait", eng, nm, v))

    def gmax(*guards):
        out = {}
        for g in guards:
            if g is None:
                continue
            nm, v = g
            out[nm] = max(out.get(nm, 0), v)
        return [(nm, v) for nm, v in out.items()]

    def waits(eng, *guards):
        for g in gmax(*guards):
            wait(eng, g)

    for (dst, src) in ((wstage[0][:, 0:768], xpw_d),
                       (wstage[1][0:64, 0:1024], dtw_d),
                       (wstage[0][:, 768:832], gw_d),
                       (acole, alog_d),
                       (dcol, dvec_d),
                       (dtbcol, dtb_d),
                       (identb_sb, identb_d),
                       (identf_sb, identf_d),
                       (onesb_sb, onesb_d)):
        t_const = em("sync", "dma_start", inc="mCst", out=dst, in_=src)
    g_lb = em("gpsimd", "dma_start", inc="mLb", out=lb_bc, in_=lb_d.partition_broadcast(P))
    wait("scalar", t_const)
    em("scalar", "copy", out=xpw_bf, in_=wstage[0][:, 0:768])
    em("scalar", "copy", out=dtw_bf, in_=wstage[1][0:64, 0:1024])
    em("scalar", "copy", out=gw_bf, in_=wstage[0][:, 768:832])
    t_sc = em("scalar", "activation", inc="eS", out=acolx, in_=acole, func=AF.Exp)
    wait("vector", t_sc)
    em("vector", "tensor_scalar_mul", acole, acolx, -1.0)
    em("vector", "memset", epscol, 1e-5)
    em("vector", "memset", ones8, 1.0, inc="eV")
    acol = acole
    consts_eS = t_sc
    consts_eV = ("eV", cnt["eV"])

    cast_st = {"stage": [consts_eS, consts_eS], "wcw": [None, None, None]}
    cast_final = [None, None, None]

    def cast_gen():
        for pid in PASS_ORDER:
            for m in range(3):
                for hf in range(4):
                    si = (4 * m + hf) % 2
                    waits("sync", cast_st["stage"][si])
                    mst = em("sync", "dma_start", inc=f"mStg{si}", out=wstage[si],
                             in_=ws_d[pid, m, :, hf * 1024:(hf + 1) * 1024])
                    waits("scalar", mst, cast_st["wcw"][m])
                    cast_st["stage"][si] = em("scalar", "copy", inc="eS",
                                              out=w_bf[m][:, hf * 1024:(hf + 1) * 1024],
                                              in_=wstage[si])
                    yield
                wl = em("scalar", "copy", inc="eS", out=touch[:, m:m + 1],
                        in_=w_bf[m][:, 4095:4096])
                wait("gpsimd", wl)
                wcw = em("gpsimd", "dma_start", inc=f"mWC{m}",
                         out=wcache_d[pid, m], in_=w_bf[m])
                cast_st["wcw"][m] = wcw
                cast_final[m] = wcw
                yield

    wc_final = [None, None, None]

    xload = [None] * NT
    s3t = [None] * NT
    v1t = [None] * NT
    s5t = [None] * NT
    v2t = [None] * NT
    s6t = [None] * NT
    t1t = [None] * NT
    evt = [None] * NT

    def p0a_load(k):
        if k >= 3:
            wait("sync", s6t[k - 3])
        xload[k] = em("sync", "dma_start", inc=f"mX{k % 3}", out=xtile[k % 3],
                      in_=x_d[k * P:(k + 1) * P, :])

    def p0a_A(k):
        wait("scalar", xload[k])
        em("scalar", "activation", out=obuf[0], in_=xtile[k % 3], func=AF.Square,
           accum_out=sumsq[:, k:k + 1])
        em("scalar", "activation", out=obuf[0], in_=xtile[k % 3], func=AF.Copy,
           scale=float(1.0 / 1024.0), accum_out=sumx[:, k:k + 1])
        s3t[k] = em("scalar", "activation", inc="eS", out=m2c[:, k:k + 1],
                    in_=sumx[:, k:k + 1], func=AF.Square)
        wait("vector", s3t[k])
        v1t[k] = em("vector", "tensor_scalar", inc="eV", out=varc[:, k:k + 1],
                    in0=sumsq[:, k:k + 1], scalar1=float(1.0 / 1024.0),
                    scalar2=m2c[:, k:k + 1], op0=ALU.mult, op1=ALU.subtract)

    def p0a_B(k):
        wait("scalar", v1t[k])
        waits("scalar", consts_eV)
        em("scalar", "activation", out=lnvc[:, k:k + 1], in_=varc[:, k:k + 1],
           func=AF.Ln, bias=epscol)
        s5t[k] = em("scalar", "activation", inc="eS", out=invc[:, k:k + 1],
                    in_=lnvc[:, k:k + 1], func=AF.Exp, scale=-0.5)

    def p0a_C(k):
        waits("vector", s5t[k], evt[k - 2] if k >= 2 else None)
        s6t[k] = em("vector", "tensor_scalar", inc="eV", out=xn_tm[k % 2],
                    in0=xtile[k % 3], scalar1=sumx[:, k:k + 1],
                    scalar2=invc[:, k:k + 1], op0=ALU.subtract, op1=ALU.mult)
        pst = ps1 if (k % 2 == 0) else ps2
        waits("tensor", s6t[k], evt[k - 2] if k >= 2 else None)
        for h in range(HCH):
            last = (h == HCH - 1)
            t1t[k] = em("tensor", "transpose", inc="eT" if last else None,
                        out=pst[:, h * 128:(h + 1) * 128],
                        in_=xn_tm[k % 2][:, h * 128:(h + 1) * 128],
                        identity=identb_sb) or t1t[k]
        wait("vector", t1t[k])
        for h in range(HCH):
            last = (h == HCH - 1)
            evt[k] = em("vector", "tensor_copy", inc="eV" if last else None,
                        out=xn_fm[h][:, k * 128:(k + 1) * 128],
                        in_=pst[:, h * 128:(h + 1) * 128]) or evt[k]

    cg = cast_gen()
    cg_alive = True
    for k in range(NT + 2):
        if k < NT:
            p0a_load(k)
            p0a_A(k)
        if 1 <= k <= NT:
            p0a_B(k - 1)
        if k >= 2:
            p0a_C(k - 2)
        if cg_alive:
            for _ in range(5):
                try:
                    next(cg)
                except StopIteration:
                    cg_alive = False
                    break
    while cg_alive:
        try:
            next(cg)
        except StopIteration:
            cg_alive = False
    for m in range(3):
        wc_final[m] = (f"mWC{m}", cnt[f"mWC{m}"])
    p0a_S_done = ("eS", cnt["eS"])
    p0a_V_done = ("eV", cnt["eV"])

    lgt = [None] * NT
    sct = [None] * NT
    tkt = [None] * NT
    selevt = [None] * (NT // 4)
    selrow_dma = [None] * (NT // 4)

    def gate_mm(k):
        psb = ps1f if (k % 2 == 0) else ps2f
        waits("tensor", evt[NT - 1] if k < 2 else lgt[k - 2], evt[k], consts_eS)
        t = None
        for h in range(HCH):
            last = (h == HCH - 1)
            t = em("tensor", "matmul", inc="eT" if last else None,
                   out=psb[:, 0:8], lhsT=xn_fm[h][:, k * 128:(k + 1) * 128],
                   rhs=gw_bf[:, h * 8:(h + 1) * 8],
                   start=(h == 0), stop=last) or t
        return t

    def topk_seq(k, j):
        sc = scstore[:, k * 8:(k + 1) * 8]
        sel = selstore[:, k * 8:(k + 1) * 8]
        return [
            ("reduce_max", dict(out=m12[:, 2 * k:2 * k + 1], in_=sc, axis=AX.X)),
            ("tensor_scalar", dict(out=teq[j], in0=sc, scalar1=m12[:, 2 * k:2 * k + 1],
                                   scalar2=None, op0=ALU.is_equal)),
            ("tensor_tensor_scan", dict(out=tcum[j], data0=ones8, data1=teq[j],
                                        initial=0.0, op0=ALU.mult, op1=ALU.add)),
            ("tensor_scalar", dict(out=tcum2[j], in0=tcum[j], scalar1=1.0,
                                   scalar2=None, op0=ALU.is_equal)),
            ("tensor_mul", dict(out=tf1[j], in0=teq[j], in1=tcum2[j])),
            ("scalar_tensor_tensor", dict(out=ts2[j], in0=tf1[j], scalar=-1e30,
                                          in1=sc, op0=ALU.mult, op1=ALU.add)),
            ("reduce_max", dict(out=m12[:, 2 * k + 1:2 * k + 2], in_=ts2[j], axis=AX.X)),
            ("tensor_scalar", dict(out=teq[j], in0=ts2[j],
                                   scalar1=m12[:, 2 * k + 1:2 * k + 2],
                                   scalar2=None, op0=ALU.is_equal)),
            ("tensor_tensor_scan", dict(out=tcum[j], data0=ones8, data1=teq[j],
                                        initial=0.0, op0=ALU.mult, op1=ALU.add)),
            ("tensor_scalar", dict(out=tcum2[j], in0=tcum[j], scalar1=1.0,
                                   scalar2=None, op0=ALU.is_equal)),
            ("tensor_mul", dict(out=twn[j], in0=teq[j], in1=tcum2[j])),
            ("tensor_add", dict(out=sel, in0=tf1[j], in1=twn[j])),
            ("tensor_copy", dict(out=wsel[:, k * 16 + 8:k * 16 + 16], in_=sel)),
        ]

    for pr in range(NT // 2):
        a, b = 2 * pr, 2 * pr + 1
        ta = gate_mm(a)
        tb = gate_mm(b)
        waits("vector", ta, g_lb, consts_eV)
        lgt[a] = em("vector", "tensor_add", inc="eV", out=lg2[0], in0=ps1f[:, 0:8], in1=lb_bc)
        wait("vector", tb)
        lgt[b] = em("vector", "tensor_add", inc="eV", out=lg2[1], in0=ps2f[:, 0:8], in1=lb_bc)
        wait("scalar", lgt[a])
        em("scalar", "activation", out=scstore[:, a * 8:(a + 1) * 8], in_=lg2[0], func=AF.Sigmoid)
        wait("scalar", lgt[b])
        sct[a] = sct[b] = em("scalar", "activation", inc="eS",
                             out=scstore[:, b * 8:(b + 1) * 8], in_=lg2[1], func=AF.Sigmoid)
        wait("vector", sct[a])
        qa, qb = topk_seq(a, 0), topk_seq(b, 1)
        for i, ((ma_, kwa), (mb_, kwb)) in enumerate(zip(qa, qb)):
            last = (i == len(qa) - 1)
            em("vector", ma_, **kwa)
            tkt[a] = tkt[b] = em("vector", mb_, inc="eV" if last else None, **kwb) or tkt[a]
        for k in (a, b):
            g4 = k // 4
            waits("tensor", tkt[k], selevt[g4 - 1] if (k % 4 == 0 and g4 >= 1) else None)
            tt = em("tensor", "transpose", inc="eT",
                    out=psyb[0:8, (k % 4) * 128:(k % 4 + 1) * 128],
                    in_=wsel[:, k * 16 + 8:k * 16 + 16], identity=identb_sb)
            if k % 4 == 3:
                waits("scalar", tt, selevt[g4 - 2] and None)
                if g4 >= 2:
                    wait("scalar", selrow_dma[g4 - 2])
                selevt[g4] = em("scalar", "copy", inc="eS",
                                out=selrows_sb[g4 % 2],
                                in_=psyb[0:8, 0:512])
                wait("gpsimd", selevt[g4])
                selrow_dma[g4] = em("gpsimd", "dma_start", inc=f"mSR{g4 % 2}",
                   out=selrows_d[:, g4 * 512:(g4 + 1) * 512], in_=selrows_sb[g4 % 2])

    vden = em("vector", "scalar_tensor_tensor", inc="eV", out=den32,
              in0=m12[:, 0:64:2], scalar=1e-8, in1=m12[:, 1:64:2],
              op0=ALU.add, op1=ALU.add)
    wait("scalar", vden)
    em("scalar", "activation", out=lnvc, in_=den32, func=AF.Ln)
    srin = em("scalar", "activation", inc="eS", out=rin32, in_=lnvc, func=AF.Exp,
              scale=-1.0)
    wait("vector", srin)
    wt = [None] * NT
    for pr in range(NT // 2):
        a, b = 2 * pr, 2 * pr + 1
        for j, k in ((0, a), (1, b)):
            em("vector", "tensor_mul", out=twn[j], in0=selstore[:, k * 8:(k + 1) * 8],
               in1=scstore[:, k * 8:(k + 1) * 8])
        for j, k in ((0, a), (1, b)):
            wt[k] = em("vector", "tensor_scalar", inc="eV" if k == b else None,
                       out=wsel[:, k * 16:k * 16 + 8], in0=twn[j],
                       scalar1=rin32[:, k:k + 1], scalar2=None, op0=ALU.mult)
        wt[a] = wt[b]
    for k in range(NT):
        if k == 0:
            wait("tensor", wt[NT - 1])
        tpart = em("tensor", "matmul", inc="eT" if k == NT - 1 else None,
                   out=psy[0][0:16, 0:1], lhsT=wsel[:, k * 16:(k + 1) * 16],
                   rhs=onesb_sb, start=(k == 0), stop=(k == NT - 1))
    wait("scalar", tpart)
    spart = em("scalar", "copy", inc="eS", out=ccsb, in_=psy[0][0:16, 0:1])
    wait("gpsimd", spart)
    em("gpsimd", "dma_start", inc="mCC", out=ccin_d.ap().rearrange("1 p -> p 1"), in_=ccsb)
    wait("gpsimd", ("mCC", cnt["mCC"]))
    gcc = em("gpsimd", "collective_compute", "AllReduce", ALU.add, inc="eG",
             replica_groups=[list(range(8))], ins=[ccin_d.ap().opt()],
             outs=[ccout_d.ap().opt()])
    wait("gpsimd", gcc)
    em("gpsimd", "dma_start", inc="mCC", out=wstat2[:, 0:1],
       in_=ccout_d.ap()[0:1, 0:8].rearrange("1 p -> p 1"))
    mws = em("gpsimd", "dma_start", inc="mCC", out=wstat2[:, 1:2],
             in_=ccout_d.ap()[0:1, 8:16].rearrange("1 p -> p 1"))
    wait("vector", mws)
    vcnt = em("vector", "tensor_scalar_max", cntc, wstat2[:, 1:2], 1.0, inc="eV")
    wait("scalar", vcnt)
    em("scalar", "activation", out=lncnt, in_=cntc, func=AF.Ln)
    srcnt = em("scalar", "activation", inc="eS", out=rcnt, in_=lncnt, func=AF.Exp,
               scale=-1.0)
    wait("vector", srcnt)
    vwm = em("vector", "scalar_tensor_tensor", inc="eV", out=wmean8,
             in0=wstat2[:, 0:1], scalar=0.5, in1=rcnt, op0=ALU.mult, op1=ALU.mult)
    wait("gpsimd", vwm)
    em("gpsimd", "dma_start", inc="mWM8", out=wm8_d.rearrange("1 p -> p 1"), in_=wmean8)
    wait("gpsimd", ("mWM8", cnt["mWM8"]))
    for e in range(8):
        em("gpsimd", "dma_start", inc="mWMB", out=wmb[:, e:e + 1],
           in_=wm8_d[0:1, e:e + 1].partition_broadcast(P))
    g_wmb = ("mWMB", cnt["mWMB"])
    g_selrow = [("mSR0", cnt["mSR0"]), ("mSR1", cnt["mSR1"])]

    p_b.append(wsel[:, 0:512])
    p_b.append(scstore.bitcast(BF16)[:, 0:512])
    da_b.append(wstage[0][:, 0:512])
    da_b.append(wstage[0][:, 512:1024])
    st = {
        "dr": [None, None], "bcsbw": [None, None],
        "bcast": [None] * 3, "ccast": [None] * 3,
        "delta": [None, None], "u": [None, None], "da": [None] * 2,
        "p": [None] * 2, "xndT": [None, None],
        "bank0": spart, "bank6": spart, "bank7": selevt[-1],
        "pst": [tpart, tpart],
        "bank3": None, "bank3b": lgt[NT - 1], "bank3v": None, "bank4": None,
        "bank4b": lgt[NT - 1], "bank5": None,
        "g": [None, None], "v": [None] * 4, "mb": [None, None],
        "wbf_used": [("mWC0", cnt["mWC0"]), ("mWC1", cnt["mWC1"]),
                     ("mWC2", cnt["mWC2"])],
        "xt": [s6t[30], s6t[31], s6t[29]],
        "ob": [p0a_S_done],
        "b12": None, "epi_t": None,
    }

    def mamba_gen(q):
        for ci in range(2):
            yield from mamba_chunk(q, ci)

    def mamba_chunk(q, ci):
        c = 2 * q + ci
        tok0 = c * CH
        ji = c % 2
        waits("tensor", st["bank0"], p0a_V_done)
        txp = None
        for h in range(HCH):
            last = (h == HCH - 1)
            txp = em("tensor", "matmul", inc="eT" if last else None,
                     out=ps0[0:96, :], lhsT=xpw_bf[:, h * 96:(h + 1) * 96],
                     rhs=xn_fm[h][:, tok0:tok0 + CH],
                     start=(h == 0), stop=last) or txp
        waits("scalar", txp, st["dr"][ji], st["bcsbw"][ji])
        em("scalar", "copy", out=dr_b[ji], in_=ps0[0:64, :])
        sdr = em("scalar", "copy", inc="eS", out=bcsb[ji], in_=ps0[64:96, :])
        st["bank0"] = sdr
        wait("gpsimd", sdr)
        gw_bc = em("gpsimd", "dma_start", inc=f"mBcb{ji}", out=bcb_d[c % 8], in_=bcsb[ji])
        st["bcsbw"][ji] = gw_bc
        yield
        for pair in range(4):
            dAc, dBc = 2 * pair, 2 * pair + 1
            for di, dch in ((0, dAc), (1, dBc)):
                waits("tensor", st["bank0"], sdr)
                td = em("tensor", "matmul", inc="eT", out=ps0[:, :],
                        lhsT=dtw_bf[:, dch * 128:(dch + 1) * 128], rhs=dr_b[ji],
                        start=True, stop=True)
                st["dr"][ji] = td
                waits("scalar", td, st["u"][di])
                em("scalar", "activation", out=t1_b[di], in_=ps0[:, :], func=AF.Exp,
                   bias=dtbcol[:, dch:dch + 1])
                sdelta = em("scalar", "activation", inc="eS", out=delta_b[di],
                            in_=t1_b[di], func=AF.Ln, bias=1.0)
                st["bank0"] = sdelta
                wait("vector", sdelta)
                st["u"][di] = em("vector", "tensor_mul", inc="eV", out=u_b[di],
                                 in0=delta_b[di], in1=xn_fm[dch][:, tok0:tok0 + CH])
                waits("scalar", st["xndT"][di])
                sxnd = em("scalar", "activation", inc="eS", out=xnd_b[di],
                          in_=xn_fm[dch][:, tok0:tok0 + CH], func=AF.Copy,
                          scale=dcol[:, dch:dch + 1])
                st["xnd"] = sxnd
                waits("tensor", sxnd, st["bank6"] if di == 0 else st["bank7"])
                st["xndT"][di] = em("tensor", "matmul", inc="eT", out=psy[di],
                                    lhsT=identb_sb, rhs=xnd_b[di],
                                    start=True, stop=False)
            yield
            svn = [None] * (N16 + 1)
            stn = [None] * (N16 + 1)
            ssn = [None] * (N16 + 1)

            def emit_da(n):
                waits("scalar", svn[n - 2] if n >= 2 else st["da"][0])
                em("scalar", "activation", out=da_b[(n % 2) * 2],
                   in_=delta_b[0], func=AF.Exp,
                   scale=acol[:, dAc * 16 + n:dAc * 16 + n + 1])
                ssn[n] = em("scalar", "activation", inc="eS",
                            out=da_b[(n % 2) * 2 + 1], in_=delta_b[1], func=AF.Exp,
                            scale=acol[:, dBc * 16 + n:dBc * 16 + n + 1])

            emit_da(0)
            for n in range(N16):
                bi = n % 3
                waits("gpsimd", gw_bc, st["bcast"][bi])
                em("gpsimd", "dma_start", inc=f"mBC{bi}", out=bcast[bi],
                   in_=bcb_d[c % 8][n:n + 1, :].partition_broadcast(P))
                mbc = em("gpsimd", "dma_start", inc=f"mBC{bi}", out=ccast[bi],
                         in_=bcb_d[c % 8][16 + n:17 + n, :].partition_broadcast(P))
                if n + 1 < N16:
                    emit_da(n + 1)
                waits("vector", ssn[n],
                      stn[n - 2] if n >= 2 else st["pst"][n])
                waits("vector", mbc)
                for di in (0, 1):
                    em("vector", "tensor_mul", out=dbu_b, in0=u_b[di], in1=bcast[bi])
                    em("vector", "tensor_tensor_scan",
                       out=hpair[:, di * CH:(di + 1) * CH],
                       data0=da_b[(n % 2) * 2 + di], data1=dbu_b,
                       initial=hstate[:, (dAc + di) * 16 + n:(dAc + di) * 16 + n + 1],
                       op0=ALU.mult, op1=ALU.add)
                    em("vector", "tensor_mul", out=p_b[(2 * n + di) % 4],
                       in0=hpair[:, di * CH:(di + 1) * CH], in1=ccast[bi])
                svn[n] = em("vector", "tensor_copy", inc="eV",
                            out=hstate[:, 32 * pair + n:32 * pair + n + 17:16],
                            in_=hpair[:, CH - 1:2 * CH:CH])
                st["bcast"][bi] = svn[n]
                if n == N16 - 1:
                    st["da"][0] = svn[N16 - 2]
                wait("tensor", svn[n])
                em("tensor", "matmul", out=psy[0], lhsT=identb_sb,
                   rhs=p_b[(2 * n) % 4], start=False, stop=(n == N16 - 1))
                stn[n] = em("tensor", "matmul", inc="eT", out=psy[1],
                            lhsT=identb_sb, rhs=p_b[(2 * n + 1) % 4],
                            start=False, stop=(n == N16 - 1))
                yield
            st["pst"] = [stn[N16 - 2], stn[N16 - 1]]
            for di, dch in ((0, dAc), (1, dBc)):
                wait("vector", stn[N16 - 1])
                vacc = em("vector", "tensor_tensor", inc="eV",
                          out=acc[dch][:, ci * CH:(ci + 1) * CH],
                          in0=acc[dch][:, ci * CH:(ci + 1) * CH], in1=psy[di],
                          op=ALU.add)
                if di == 0:
                    st["bank6"] = vacc
                else:
                    st["bank7"] = vacc

    def ffn_gen(q):
        for pi, pid in enumerate(PASS_ORDER):
            yield from ffn_pass(q, pi, pid)

    def ffn_pass(q, pi, pid):
        masked = pid < 8
        wload = [None, None, None]
        for m in range(3):
            waits("sync", st["wbf_used"][m], wc_final[m])
            wload[m] = em("sync", "dma_start", inc=f"mWB{m}", out=w_bf[m],
                          in_=wcache_d[pid, m])
        yield
        if masked:
            mi = pi % 2
            waits("gpsimd", g_selrow[0], g_selrow[1], st["mb"][mi])
            mmb = em("gpsimd", "dma_start", inc=f"mMB{mi}", out=mbt[mi],
                     in_=selrows_d[pid:pid + 1, q * QT:(q + 1) * QT].partition_broadcast(P))
            waits("scalar", g_wmb)
        for ti in range(2):
            tok0 = q * QT + ti * 512
            for ich in range(4):
                gi = ich % 2
                gpsum = psg if gi == 0 else ps1f
                upsum = psu if gi == 0 else ps2f
                bk3 = "bank3" if gi == 0 else "bank3b"
                bk4 = "bank4" if gi == 0 else "bank4b"
                waits("tensor", wload[0], wload[1], st[bk3],
                      st["bank3v"] if gi == 0 else st["b12"], p0a_V_done)
                tg = None
                for h in range(HCH):
                    last = (h == HCH - 1)
                    tg = em("tensor", "matmul", inc="eT" if last else None, out=gpsum,
                            lhsT=w_bf[0][:, h * 512 + ich * 128:h * 512 + (ich + 1) * 128],
                            rhs=xn_fm[h][:, tok0:tok0 + 512],
                            start=(h == 0), stop=last) or tg
                waits("tensor", st[bk4])
                tu = None
                for h in range(HCH):
                    last = (h == HCH - 1)
                    tu = em("tensor", "matmul", inc="eT" if last else None, out=upsum,
                            lhsT=w_bf[1][:, h * 512 + ich * 128:h * 512 + (ich + 1) * 128],
                            rhs=xn_fm[h][:, tok0:tok0 + 512],
                            start=(h == 0), stop=last) or tu
                waits("scalar", tg, st["g"][gi])
                em("scalar", "copy", out=g_b[gi], in_=gpsum)
                st[bk3] = em("scalar", "activation", inc="eS", out=tt_b[gi],
                             in_=gpsum, func=AF.Tanh, scale=0.5)
                waits("scalar", tu)
                if masked:
                    st[bk4] = em("scalar", "activation", inc="eS", out=u2_b[gi],
                                 in_=upsum, func=AF.Copy, scale=wmb[:, pid:pid + 1])
                else:
                    st[bk4] = em("scalar", "activation", inc="eS", out=u2_b[gi],
                                 in_=upsum, func=AF.Copy, scale=0.5)
                waits("vector", st[bk4], st["v"][ich],
                      mmb if (masked and ti == 0 and ich == 0) else None)
                em("vector", "tensor_mul", out=vtmp, in0=g_b[gi], in1=u2_b[gi])
                if masked:
                    em("vector", "scalar_tensor_tensor", out=vtmp2, in0=tt_b[gi],
                       scalar=1.0, in1=mbt[pi % 2][:, ti * 512:(ti + 1) * 512],
                       op0=ALU.add, op1=ALU.mult)
                    vv = em("vector", "tensor_mul", inc="eV", out=v_b[ich],
                            in0=vtmp, in1=vtmp2)
                else:
                    em("vector", "tensor_scalar_add", vtmp2, tt_b[gi], 1.0)
                    vv = em("vector", "tensor_mul", inc="eV", out=v_b[ich],
                            in0=vtmp, in1=vtmp2)
                st["g"][gi] = vv
                if masked and ti == 1 and ich == 3:
                    st["mb"][pi % 2] = vv
                if ich == 3 and ti == 1:
                    st["wbf_used"][0] = ("eT", cnt["eT"])
                    st["wbf_used"][1] = ("eT", cnt["eT"])
                if ich == 3:
                    vlast_t = vv
                yield
            vlast = vlast_t
            for h in range(HCH):
                bank = psd if (h % 2 == 0) else psg
                bkey = "bank5" if (h % 2 == 0) else "bank3v"
                waits("tensor", vlast, st[bkey], wload[2],
                      st["bank3"] if h % 2 == 1 else None)
                tdn = None
                for ich in range(4):
                    last = (ich == 3)
                    tdn = em("tensor", "matmul", inc="eT" if last else None, out=bank,
                             lhsT=w_bf[2][:, ich * 1024 + h * 128:ich * 1024 + (h + 1) * 128],
                             rhs=v_b[ich], start=(ich == 0), stop=last) or tdn
                if h == HCH - 1:
                    for ich in range(4):
                        st["v"][ich] = tdn
                    if ti == 1:
                        st["wbf_used"][2] = tdn
                wait("vector", tdn)
                st[bkey] = em("vector", "tensor_tensor", inc="eV",
                              out=acc[h][:, ti * 512:(ti + 1) * 512],
                              in0=acc[h][:, ti * 512:(ti + 1) * 512], in1=bank,
                              op=ALU.add)
                if h % 2 == 1:
                    yield

    def epilogue(q, acc_done):
        for tb in range(8):
            gt = q * 8 + tb
            xi = tb % 3
            waits("sync", st["xt"][xi])
            mxr = em("sync", "dma_start", inc=f"mX{xi}", out=xtile[xi],
                     in_=x_d[gt * P:(gt + 1) * P, :])
            waits("tensor", acc_done, st["b12"], st["bank3b"], st["bank4b"])
            ttr = None
            for h in range(HCH):
                psb = ps1f if h < 4 else ps2f
                last = (h == HCH - 1)
                ttr = em("tensor", "transpose", inc="eT" if last else None,
                         out=psb[:, (h % 4) * 128:(h % 4 + 1) * 128],
                         in_=acc[h][:, tb * 128:(tb + 1) * 128],
                         identity=identf_sb) or ttr
            oi = 0
            waits("vector", ttr, mxr, st["ob"][oi])
            em("vector", "tensor_tensor", out=obuf[oi][:, 0:512], in0=ps1f,
               in1=xtile[xi][:, 0:512], op=ALU.add)
            vep = em("vector", "tensor_tensor", inc="eV", out=obuf[oi][:, 512:1024],
                     in0=ps2f, in1=xtile[xi][:, 512:1024], op=ALU.add)
            st["b12"] = vep
            st["xt"][xi] = vep
            wait("sync", vep)
            em("sync", "dma_start", inc="mO", out=out_d[gt * P:(gt + 1) * P, :],
               in_=obuf[oi])
            st["ob"][oi] = ("mO", cnt["mO"])

    for q in range(NQ):
        for h in range(HCH):
            waits("vector", st["epi_t"])
            em("vector", "memset", acc[h], 0.0)
        if q % 2 == 0:
            em("vector", "memset", hstate, 0.0)
        em("vector", "nop", inc="eV")
        INTERLEAVE = globals().get("_INTERLEAVE", True)
        gm = mamba_gen(q)
        gf = ffn_gen(q)
        alive_m, alive_f = True, True
        while alive_m or alive_f:
            if alive_f:
                for _ in range(12 if INTERLEAVE else 10 ** 9):
                    try:
                        next(gf)
                    except StopIteration:
                        alive_f = False
                        break
            if alive_m:
                for _ in range(8 if INTERLEAVE else 10 ** 9):
                    try:
                        next(gm)
                    except StopIteration:
                        alive_m = False
                        break
        acc_done = ("eV", cnt["eV"])
        epilogue(q, acc_done)
        st["epi_t"] = ("eT", cnt["eT"])

    wait("gpsimd", ("mO", cnt["mO"]))

    block = En(nc.Block())

    @block.sync
    def _(e):
        for f in ops["sync"]:
            f(e)

    @block.gpsimd
    def _(e):
        for f in ops["gpsimd"]:
            f(e)

    @block.scalar
    def _(e):
        for f in ops["scalar"]:
            f(e)

    @block.vector
    def _(e):
        for f in ops["vector"]:
            f(e)

    @block.tensor
    def _(e):
        for f in ops["tensor"]:
            f(e)

    stack.pop_all()
    return nc


def _host_prep(inputs):
    f32 = np.float32
    gw8 = np.asarray(inputs["L0_ffn_expert_gate_w"], f32)
    uw8 = np.asarray(inputs["L0_ffn_expert_up_w"], f32)
    dw8 = np.asarray(inputs["L0_ffn_expert_down_w"], f32)
    sg = np.asarray(inputs["L0_ffn_shared_gate_w"], f32)
    su = np.asarray(inputs["L0_ffn_shared_up_w"], f32)
    sd = np.asarray(inputs["L0_ffn_shared_down_w"], f32)

    def packT(w):
        wt = np.ascontiguousarray(w.T)
        I, O = wt.shape
        return np.ascontiguousarray(
            wt.reshape(I // 128, 128, O).transpose(1, 0, 2).reshape(128, (I // 128) * O))

    wstream = np.empty((NPASS, 3, P, 4096), f32)
    for e in range(8):
        wstream[e, 0] = packT(gw8[e])
        wstream[e, 1] = packT(uw8[e])
        wstream[e, 2] = packT(dw8[e])
    wstream[8, 0] = packT(sg[0:512])
    wstream[8, 1] = packT(su[0:512])
    wstream[8, 2] = packT(sd[:, 0:512])
    wstream[9, 0] = packT(sg[512:1024])
    wstream[9, 1] = packT(su[512:1024])
    wstream[9, 2] = packT(sd[:, 512:1024])

    xpw = packT(np.asarray(inputs["L0_attn_x_proj_w"], f32))
    dtw = np.ascontiguousarray(np.asarray(inputs["L0_attn_dt_proj_w"], f32).T)
    alog = np.ascontiguousarray(
        np.asarray(inputs["L0_attn_A_log"], f32).reshape(8, 128, 16)
        .transpose(1, 0, 2).reshape(128, 128))
    dvec = np.ascontiguousarray(
        np.asarray(inputs["L0_attn_D"], f32).reshape(8, 128).T)
    dtb = np.ascontiguousarray(
        np.asarray(inputs["L0_attn_dt_proj_b"], f32).reshape(8, 128).T)
    gwm = packT(np.asarray(inputs["L0_ffn_gate_w"], f32))
    lb = np.asarray(inputs["L0_ffn_lb_bias"], f32).reshape(1, 8)
    identb = np.eye(P, dtype=BF)
    identf = np.eye(P, dtype=np.float32)
    onesb = np.ones((P, 1), dtype=BF)
    return dict(wstream=wstream, xpw=xpw, dtw=dtw, alog=alog, dvec=dvec, dtb=dtb,
                gw=gwm, lb=lb, identb=identb, identf=identf, onesb=onesb)


def kernel(**inputs):
    x = np.asarray(inputs["L0_attn_x"], np.float32)
    shared = _host_prep(inputs)
    nc = build_program()
    in_maps = []
    for c in range(8):
        m = dict(shared)
        m["x"] = np.ascontiguousarray(x[2 * c:2 * c + 2].reshape(TCORE, 1024))
        in_maps.append(m)
    res = run_bass_kernel_spmd(nc, in_maps, core_ids=list(range(8)))
    out = np.concatenate([r["out"].reshape(2, 2048, 1024) for r in res.results], axis=0)
    return out.astype(np.float32)
